# version 30
# baseline (speedup 1.0000x reference)
import os
import numpy as np
import ml_dtypes
from contextlib import ExitStack

import concourse.bass as bass
import concourse.mybir as mybir
import concourse.bass_utils as _bu
from concourse.bass_utils import run_bass_kernel_spmd

if not getattr(_bu, "_ldwopt_patched", False):
    _orig_run_command = _bu.run_command

    def _run_command_ldwopt(argv, **kw):
        argv = ["--enable-ldw-opt=true" if a == "--enable-ldw-opt=false"
                else a for a in argv]
        return _orig_run_command(argv, **kw)

    _bu.run_command = _run_command_ldwopt
    _bu._ldwopt_patched = True

F32 = mybir.dt.float32
BF16 = mybir.dt.bfloat16
AF = mybir.ActivationFunctionType
ALU = mybir.AluOpType
AX = mybir.AxisListType

D = 1024
H = 16
S = 2048
B = 32
NCORES = 8
TOK = (B // NCORES) * S
TILE = 512
NTILE = TOK // TILE
HB = 32
HCH = 5
HTOK = TILE + 2 * HB
CH = TILE // 128
WIN = 192
NEG = -1.0e30
EPS_LN = 1.0e-5
EPS_Z = 1.0e-30
WH = 8

MARKS = {}


class Trace:
    def __init__(self):
        self.ops = []
        self.need_inc = set()

    def op(self, eng, fn, deps=(), dma_sem=None, strict=False):
        idx = len(self.ops)
        deps = [d for d in deps if d is not None]
        self.ops.append(dict(eng=eng, fn=fn, deps=deps, dma=dma_sem,
                             strict=strict))
        for d in deps:
            self.need_inc.add(d)
        return idx


class Res:
    def __init__(self):
        self.writers = []
        self.readers = []

    def write(self):
        deps = self.writers + self.readers
        self.writers = []
        self.readers = []
        return deps

    def read(self):
        return list(self.writers)


def build_program():
    nc = bass.Bass("TRN2", target_bir_lowering=False,
                   detect_race_conditions=False,
                   dynamic_dma_scratch_size=12288)

    def din(name, shape, dt=F32):
        return nc.dram_tensor(name, shape, dt, kind="ExternalInput").ap()

    x_d = din("x", [TOK, D])
    y_d = nc.dram_tensor("y", [TOK, D], F32, kind="ExternalOutput").ap()
    w_d = {n: din("w" + n, [D, D]) for n in ("q", "k", "v", "o")}
    bq_d = din("bq", [128, 8])
    bk_d = din("bk", [128, 8])
    rows_d = din("rows", [65, D])
    cp_d = din("Cp", [D, D], BF16)
    sn_d = din("Sn", [D, D], BF16)
    wrT_d = din("wrT", [D, D])
    wiT_d = din("wiT", [D, D])
    masks_d = din("masks", [128, 3 * WIN], BF16)
    idb_d = din("idb", [128, 128], BF16)

    tr = Trace()

    with ExitStack() as st:
        e = st.enter_context

        wsb = {n: e(nc.sbuf_tensor("w_" + n, [128, 8 * D], BF16))
               for n in ("q", "k", "v", "o")}
        m_sb = e(nc.sbuf_tensor("m_sb", [128, 8 * D], BF16))
        masks = e(nc.sbuf_tensor("masks_sb", [128, 3 * WIN], BF16))
        idb = e(nc.sbuf_tensor("idb_sb", [128, 128], BF16))
        bq_s = e(nc.sbuf_tensor("bq_s", [128, 8], F32))
        bk_s = e(nc.sbuf_tensor("bk_s", [128, 8], F32))
        bv_bf = e(nc.sbuf_tensor("bv_bf", [1, D], BF16))
        bo_bf = e(nc.sbuf_tensor("bo_bf", [1, D], BF16))
        bf_bf = e(nc.sbuf_tensor("bf_bf", [1, D], BF16))
        ones_bf = e(nc.sbuf_tensor("ones_bf", [1, 128], BF16))
        kconst = e(nc.sbuf_tensor("kconst", [128, 4], F32))
        sumx = e(nc.sbuf_tensor("sumx", [128, 16], F32))
        sumq = e(nc.sbuf_tensor("sumq", [128, 16], F32))
        mst = e(nc.sbuf_tensor("mst", [128, 16], F32))
        msq = e(nc.sbuf_tensor("msq", [128, 16], F32))
        vst = e(nc.sbuf_tensor("vst", [128, 16], F32))
        rst = e(nc.sbuf_tensor("rst", [128, 16], F32))
        negm = e(nc.sbuf_tensor("negm", [128, 16], F32))
        zsum = e(nc.sbuf_tensor("zsum", [128, 16], F32))
        rcp = e(nc.sbuf_tensor("rcp", [128, 16], F32))

        pw = [e(nc.psum_tensor("pw%d" % w, [128, 3 * 512], F32))
              for w in range(2)]
        pyf = e(nc.psum_tensor("pyf", [128, 1024], F32))
        pxnt = pyf[:, 0:512].bitcast(BF16)
        bank = [Res() for _ in range(8)]
        BANKW = [[0, 1, 2], [3, 4, 5]]

        mb = ExitStack()
        cb = mb.enter_context(nc.sbuf_tensor("cb", [128, 8 * D], BF16))
        snb = mb.enter_context(nc.sbuf_tensor("snb", [128, 8 * D], BF16))
        wrb = mb.enter_context(nc.sbuf_tensor("wrb", [128, 8 * D], BF16))
        wib = mb.enter_context(nc.sbuf_tensor("wib", [128, 8 * D], BF16))
        stg_mb = [mb.enter_context(nc.sbuf_tensor("stgmb%d" % i, [128, D], F32))
                  for i in range(2)]
        rows_f = mb.enter_context(nc.sbuf_tensor("rows_f", [65, D], F32))
        r_stg_mb = [Res(), Res()]

        nIN = [0]
        nOUT = [0]

        def dma_in(dst_ap, src_ap, deps, sem="IN"):
            def fn(eng, dst=dst_ap, src=src_ap):
                return eng.dma_start(out=dst, in_=src)
            idx = tr.op("Y", fn, deps, dma_sem=sem)
            nIN[0] += 1
            return idx

        def dma_g(dst_ap, src_ap, deps):
            def fn(eng, dst=dst_ap, src=src_ap):
                return eng.dma_start(out=dst, in_=src)
            idx = tr.op("Y", fn, deps, dma_sem="OUT")
            nOUT[0] += 1
            return idx

        def dma_s(dst_ap, src_ap, deps, sem="QK"):
            def fn(eng, dst=dst_ap, src=src_ap):
                return eng.dma_start(out=dst, in_=src)
            return tr.op("Y", fn, deps, dma_sem=sem)

        def vop(fn, deps, strict=False):
            return tr.op("V", fn, deps, strict=strict)

        def sop(fn, deps, strict=False):
            return tr.op("S", fn, deps, strict=strict)

        def top(fn, deps):
            return tr.op("T", fn, deps)

        d_mask = dma_in(masks[:], masks_d, [])
        d_idb = dma_in(idb[:], idb_d, [])
        d_bq = dma_in(bq_s[:], bq_d, [])
        d_bk = dma_in(bk_s[:], bk_d, [])
        d_rows = dma_in(rows_f[:], rows_d, [])

        i_ones = vop(lambda v: v.memset(ones_bf[:], 1.0), [])
        vop(lambda v: v.memset(kconst[:, 0:1], EPS_LN), [])
        vop(lambda v: v.memset(kconst[:, 1:2], -0.5), [])
        vop(lambda v: v.memset(kconst[:, 2:3], 0.125), [])
        i_const = vop(lambda v: v.memset(kconst[:, 3:4], -1.0), [])
        C_EPS = kconst[:, 0:1]
        C_NHALF = kconst[:, 1:2]
        C_QS = kconst[:, 2:3]
        C_NEG1 = kconst[:, 3:4]
        i_r1 = vop(lambda v: v.tensor_copy(out=bv_bf[:], in_=rows_f[0:1, :]),
                   [d_rows])
        i_r2 = vop(lambda v: v.tensor_copy(out=bo_bf[:], in_=rows_f[32:33, :]),
                   [d_rows])
        i_r3 = vop(lambda v: v.tensor_copy(out=bf_bf[:], in_=rows_f[64:65, :]),
                   [d_rows])

        def load_bf(dst_sb, src_d, extra_dst=None, gate=()):
            evs = []
            for j in range(8):
                p = j % 2
                dd = dma_in(stg_mb[p][:], src_d[128 * j:128 * (j + 1), :],
                            r_stg_mb[p].write() + list(gate),
                            sem=("IN" if p == 0 else "IN2"))
                r_stg_mb[p].writers = [dd]

                def fn(engs, j=j, p=p, dst=dst_sb):
                    return engs.activation(out=dst[:, D * j:D * (j + 1)],
                                           in_=stg_mb[p][:], func=AF.Copy,
                                           bias=0.0, scale=1.0)
                cc = sop(fn, [dd])
                r_stg_mb[p].readers.append(cc)
                evs.append(cc)
                if extra_dst is not None:
                    def fn2(engs, j=j, p=p, dst=extra_dst):
                        return engs.activation(out=dst[:, D * j:D * (j + 1)],
                                               in_=stg_mb[p][:],
                                               func=AF.Identity,
                                               bias=0.0, scale=C_NEG1)
                    c2 = sop(fn2, [dd, i_const])
                    r_stg_mb[p].readers.append(c2)
                    evs.append(c2)
            return evs

        ev_cb = [dma_in(cb[:].rearrange("p (j n) -> p j n", j=8),
                        cp_d[:].rearrange("(j p) n -> p j n", p=128), [],
                        sem="IN")]
        ev_snb = [dma_in(snb[:].rearrange("p (j n) -> p j n", j=8),
                         sn_d[:].rearrange("(j p) n -> p j n", p=128), [],
                         sem="IN2")]
        ev_wrb = load_bf(wrb, wrT_d)
        ev_wib = load_bf(wib, wiT_d, extra_dst=wsb["v"])

        def mm_product(dst_sb, lhs_a, rhs_a, lhs_b, rhs_b, deps_in):
            evs = []
            for i in range(8):
                dw = bank[6].write() + bank[7].write()
                last = None
                for ci, (lhs, rhs) in enumerate(((lhs_a, rhs_a),
                                                 (lhs_b, rhs_b))):
                    for j in range(8):
                        for hf in range(2):
                            def fmm(t, lhs=lhs, rhs=rhs, i=i, j=j, hf=hf,
                                    st_=(ci == 0 and j == 0),
                                    sp_=(ci == 1 and j == 7)):
                                return t.matmul(
                                    out=pyf[:, 512 * hf:512 * (hf + 1)],
                                    lhsT=lhs[:, D * j + 128 * i:
                                             D * j + 128 * (i + 1)],
                                    rhs=rhs[:, D * j + 512 * hf:
                                            D * j + 512 * (hf + 1)],
                                    start=st_, stop=sp_)
                            last = top(fmm, (dw + deps_in)
                                       if (ci == 0 and j == 0 and hf == 0)
                                       else [])
                bank[6].writers = [last]
                bank[7].writers = [last]

                def fev(engs, i=i, dst=dst_sb):
                    return engs.activation(
                        out=dst[:, D * i:D * (i + 1)],
                        in_=pyf[:, 0:1024], func=AF.Copy, bias=0.0, scale=1.0)
                ee = sop(fev, [last])
                bank[6].readers.append(ee)
                bank[7].readers.append(ee)
                evs.append(ee)
            return evs

        ev_w = {}
        ev_w["o"] = load_bf(wsb["o"], w_d["o"])

        ev_p1 = mm_product(wsb["q"], wrb, cb, wib, snb,
                           ev_cb + ev_snb + ev_wrb + ev_wib)
        ev_p2 = mm_product(wsb["k"], wsb["v"], cb, wrb, snb, ev_p1)
        ev_w["v"] = load_bf(wsb["v"], w_d["v"], gate=ev_p2)
        ev_m = mm_product(m_sb, cb, wsb["q"], snb, wsb["k"], ev_p2)

        mb.close()

        stg = [e(nc.sbuf_tensor("stg%d" % i, [128, D], F32)) for i in range(4)]
        r_stg = [Res() for _ in range(4)]
        SEMS_IN = ("IN", "IN2", "IN3", "IN4")
        for n in ("q", "k"):
            evs = []
            for j in range(8):
                p = j % 4
                dd = dma_in(stg[p][:], w_d[n][128 * j:128 * (j + 1), :],
                            r_stg[p].write() + ev_m,
                            sem=SEMS_IN[p])
                r_stg[p].writers = [dd]

                def fn(engs, j=j, p=p, dst=wsb[n]):
                    return engs.activation(out=dst[:, D * j:D * (j + 1)],
                                           in_=stg[p][:], func=AF.Copy,
                                           bias=0.0, scale=1.0)
                cc = sop(fn, [dd])
                r_stg[p].readers.append(cc)
                evs.append(cc)
            ev_w[n] = evs
        MARKS["mbuild"] = len(tr.ops)

        xn_fm = e(nc.sbuf_tensor("xn_fm", [128, 8 * HTOK], BF16))
        q64 = e(nc.sbuf_tensor("q64", [64, H * TILE], BF16))
        k64 = e(nc.sbuf_tensor("k64", [64, H * HTOK], BF16))
        v_tok = [e(nc.sbuf_tensor("v_tok%d" % i, [128, HCH * D], BF16))
                 for i in range(2)]
        x_bf = e(nc.sbuf_tensor("x_bf", [128, CH * D], BF16))
        xn_tok = [e(nc.sbuf_tensor("xn_tok%d" % i, [128, D], BF16))
                  for i in range(2)]
        scrap = e(nc.sbuf_tensor("scrap", [128, D], BF16))
        attn_t = [e(nc.sbuf_tensor("attn%d" % i, [128, WH * WIN], BF16))
                  for i in range(2)]
        at1_s = e(nc.sbuf_tensor("at1_s", [128, 1024], BF16))
        at2_s = e(nc.sbuf_tensor("at2_s", [64, 1024], BF16))
        at1 = [at1_s, at1_s]
        at2 = [at2_s, at2_s]
        ctx = e(nc.sbuf_tensor("ctx", [128, 1024], BF16))
        tbf = [e(nc.sbuf_tensor("tbf%d" % i, [128, 1024], BF16))
               for i in range(2)]
        qktmp = [e(nc.sbuf_tensor("qktmp%d" % i, [128, HTOK], BF16))
                 for i in range(2)]
        outsb = e(nc.sbuf_tensor("outsb", [128, D], F32))

        xn3_t = e(nc.sbuf_tensor("xn3_t", [128, D], BF16))
        xn3_f = [e(nc.sbuf_tensor("xn3_f%d" % i, [128, D], BF16))
                 for i in range(2)]

        r_stgS = r_stg
        r_xn_tok = [Res(), Res()]
        r_scrap = Res()
        r_xn3t = Res()
        r_xn3f = [Res(), Res()]
        r_xn_fm = [Res() for _ in range(HCH)]
        r_q = Res()
        r_k = Res()
        r_v = [[Res() for _ in range(HCH)] for _ in range(2)]
        r_xbf = [Res() for _ in range(CH)]
        r_attn = [Res(), Res()]
        r_at_s = Res()
        r_at = [r_at_s, r_at_s]
        r_ctx = Res()
        r_tbf = [Res(), Res()]
        r_out = Res()
        r_stats = [Res() for _ in range(HCH)]
        r_st2 = Res()
        r_st3 = Res()
        r_st4 = Res()
        r_negm = [Res(), Res()]
        r_stT = [Res(), Res()]
        r_qktmp = [Res(), Res()]

        def make_preps(t):
            pg = t % 2
            a = t * TILE
            seq_lo = (a // S) * S
            seq_hi = seq_lo + S
            xnf = xn_fm
            vtk = v_tok[pg]
            xbf = x_bf
            thunks = []

            def th_a(hc):
                p = hc % 4
                r0 = a - HB + 128 * hc
                r1 = r0 + (128 if hc < HCH - 1 else 64)
                v0, v1 = max(r0, seq_lo), min(r1, seq_hi)
                deps = r_stgS[p].write()
                dd = dma_in(stg[p][v0 - r0:v1 - r0, :], x_d[v0:v1, :], deps,
                            sem=SEMS_IN[p])
                r_stgS[p].writers = [dd]
                dm = dd
                if v0 > r0:
                    dm = vop(lambda v, p=p, n=v0 - r0:
                             v.memset(stg[p][0:n, :], 0.0), deps)
                    r_stgS[p].writers.append(dm)
                if v1 < r1:
                    dm = vop(lambda v, p=p, n0=v1 - r0, n1=r1 - r0:
                             v.memset(stg[p][n0:n1, :], 0.0), deps)
                    r_stgS[p].writers.append(dm)

                i_sum = vop(lambda v, p=p, hc=hc:
                            v.reduce_sum(out=sumx[:, hc:hc + 1], in_=stg[p][:],
                                         axis=AX.X),
                            [dd, dm] + r_stats[hc].read())

                def f_sq(engs, p=p, hc=hc):
                    return engs.activation(out=scrap[:], in_=stg[p][:],
                                           func=AF.Square, bias=0.0, scale=1.0,
                                           accum_out=sumq[:, hc:hc + 1])
                i_sq = sop(f_sq, [dd, dm])
                r_stgS[p].readers += [i_sum, i_sq]
                r_stats[hc].readers += [i_sum, i_sq]

                sdeps = [i_sum, i_sq] + r_stats[hc].write()
                i_m = vop(lambda v, hc=hc: v.tensor_scalar(
                    out=mst[:, hc:hc + 1], in0=sumx[:, hc:hc + 1],
                    scalar1=1.0 / D, scalar2=None, op0=ALU.mult), sdeps)
                i_mm = vop(lambda v, hc=hc: v.tensor_tensor(
                    out=msq[:, hc:hc + 1], in0=mst[:, hc:hc + 1],
                    in1=mst[:, hc:hc + 1], op=ALU.mult), [i_m], strict=True)
                i_var = vop(lambda v, hc=hc: v.scalar_tensor_tensor(
                    out=vst[:, hc:hc + 1], in0=sumq[:, hc:hc + 1],
                    scalar=1.0 / D, in1=msq[:, hc:hc + 1],
                    op0=ALU.mult, op1=ALU.subtract), [i_mm], strict=True)
                i_ln = sop(lambda s_, hc=hc: s_.activation(
                    out=vst[:, hc:hc + 1], in_=vst[:, hc:hc + 1],
                    func=AF.Ln, bias=C_EPS, scale=1.0), [i_var, i_const])
                i_rs = sop(lambda s_, hc=hc: s_.activation(
                    out=rst[:, hc:hc + 1], in_=vst[:, hc:hc + 1],
                    func=AF.Exp, bias=0.0, scale=C_NHALF),
                    [i_ln], strict=True)
                r_stats[hc].writers = [i_m, i_mm, i_var, i_ln, i_rs]

                pt = hc % 2
                i_ap = vop(lambda v, p=p, pt=pt, hc=hc:
                           v.tensor_scalar(out=xn_tok[pt][:], in0=stg[p][:],
                                           scalar1=mst[:, hc:hc + 1],
                                           scalar2=rst[:, hc:hc + 1],
                                           op0=ALU.subtract, op1=ALU.mult),
                           [i_rs, dd, dm] + r_xn_tok[pt].write(), strict=True)
                r_xn_tok[pt].writers = [i_ap]
                r_stgS[p].readers.append(i_ap)
                r_stats[hc].readers.append(i_ap)
                if v0 > r0:
                    i_ap = vop(lambda v, pt=pt, n=v0 - r0:
                               v.memset(xn_tok[pt][0:n, :], 0.0), [i_ap])
                    r_xn_tok[pt].writers.append(i_ap)
                if v1 < r1:
                    i_ap = vop(lambda v, pt=pt, n0=v1 - r0, n1=r1 - r0:
                               v.memset(xn_tok[pt][n0:n1, :], 0.0), [i_ap])
                    r_xn_tok[pt].writers.append(i_ap)
                a_state[hc] = i_ap

            def th_a_tr(hc):
                p = hc % 2
                i_ap = a_state[hc]
                dw = bank[6].write()
                lastT = None
                for dch in range(8):
                    lastT = top(lambda tt, p=p, dch=dch:
                                tt.transpose(
                                    out=pxnt[:, 128 * dch:128 * (dch + 1)],
                                    in_=xn_tok[p][:, 128 * dch:128 * (dch + 1)],
                                    identity=idb[:]),
                                ([i_ap, d_idb] + dw) if dch == 0 else [])
                bank[6].writers = [lastT]
                r_xn_tok[p].readers.append(lastT)

                ncol = 128 if hc < HCH - 1 else 64

                def f_ev(engs, hc=hc, ncol=ncol, xnf=xnf):
                    srcv = pxnt[:].rearrange("p (c z) -> p c z", c=8)[:, :, 0:ncol]
                    dstv = xnf[:].rearrange("p (c n) -> p c n", c=8)[
                        :, :, 128 * hc:128 * hc + ncol]
                    return engs.tensor_copy(out=dstv, in_=srcv)
                i_ev = vop(f_ev, [lastT] + r_xn_fm[hc].write())
                r_xn_fm[hc].writers = [i_ev]
                bank[6].readers.append(i_ev)

            a_state = {}
            for hc in range(HCH):
                thunks.append(lambda hc=hc: th_a(hc))
            for hc in range(HCH):
                thunks.append(lambda hc=hc: th_a_tr(hc))

            def th_a2(cs):
                for c in cs:
                    p = (HCH + c) % 4
                    deps = r_stgS[p].write()
                    dd = dma_in(stg[p][:],
                                x_d[a + 128 * c:a + 128 * (c + 1), :], deps,
                                sem=SEMS_IN[p])
                    r_stgS[p].writers = [dd]
                    i_c = vop(lambda v, p=p, c=c, xbf=xbf:
                              v.tensor_copy(out=xbf[:, D * c:D * (c + 1)],
                                            in_=stg[p][:]),
                              [dd] + r_xbf[c].write())
                    r_xbf[c].writers = [i_c]
                    r_stgS[p].readers.append(i_c)
            for cc_ in range(CH):
                thunks.append(lambda cc_=cc_: th_a2([cc_]))

            def th_vproj(vc):
                xn3v = xnf[:].rearrange("p (c n) -> p c n", c=8)
                xn_deps = list(r_xn_fm[vc].writers)
                war_v = r_v[pg][vc].write()
                nrow = 128 if vc < HCH - 1 else 64
                dw = bank[6].write() + bank[7].write()
                lastT = None
                for j in range(8):
                    for h2 in range(2):
                        view = pyf[0:nrow, 512 * h2:512 * (h2 + 1)]
                        lastT = top(
                            lambda tt, vc=vc, j=j, view=view, nrow=nrow,
                            xn3v=xn3v, h2=h2:
                            tt.matmul(
                                out=view,
                                lhsT=xn3v[:, j, 128 * vc:128 * vc + nrow],
                                rhs=wsb["v"][:, D * j + 512 * h2:
                                             D * j + 512 * (h2 + 1)],
                                start=(j == 0), stop=False),
                            (dw + xn_deps + ev_w["v"])
                            if (j == 0 and h2 == 0) else [])
                for h2 in range(2):
                    view = pyf[0:nrow, 512 * h2:512 * (h2 + 1)]
                    lastT = top(lambda tt, h2=h2, view=view, nrow=nrow:
                                tt.matmul(out=view, lhsT=ones_bf[:, 0:nrow],
                                          rhs=bv_bf[:, 512 * h2:512 * (h2 + 1)],
                                          start=False, stop=True),
                                [i_ones, i_r1])
                bank[6].writers = [lastT]
                bank[7].writers = [lastT]
                r_xn_fm[vc].readers.append(lastT)

                e0 = sop(lambda engs, vc=vc, nrow=nrow, vtk=vtk:
                         engs.activation(
                             out=vtk[0:nrow, D * vc:D * vc + 512],
                             in_=pyf[0:nrow, 0:512], func=AF.Copy,
                             bias=0.0, scale=1.0), [lastT] + war_v)
                e1 = sop(lambda engs, vc=vc, nrow=nrow, vtk=vtk:
                         engs.activation(
                             out=vtk[0:nrow, D * vc + 512:D * (vc + 1)],
                             in_=pyf[0:nrow, 512:1024], func=AF.Copy,
                             bias=0.0, scale=1.0), [lastT])
                bank[6].readers.append(e0)
                bank[7].readers.append(e1)
                r_v[pg][vc].writers = [e0, e1]

            for vc in range(HCH):
                thunks.append(lambda vc=vc: th_vproj(vc))

            def th_qk(dst64, wname, col0, ncols, bias_s, scale, res, irng):
                xn3v = xnf[:].rearrange("p (c n) -> p c n", c=8)
                xn_deps = [w_ for rr in r_xn_fm for w_ in rr.writers]
                war = res.write()
                stripes = []
                s0 = 0
                while s0 < ncols:
                    sl = min(512, ncols - s0)
                    stripes.append((s0, sl))
                    s0 += sl
                for i in irng:
                    qkt = qktmp[i % 2]
                    wart = r_qktmp[i % 2].write()
                    evs_i = []
                    dw = bank[6].write() + (bank[7].write()
                                            if len(stripes) > 1 else [])
                    lastT = None
                    for j in range(8):
                        for si, (s0, sl) in enumerate(stripes):
                            view = pyf[:, 512 * si:512 * si + sl]
                            lastT = top(
                                lambda tt, i=i, j=j, s0=s0, sl=sl, view=view,
                                wname=wname, col0=col0, xn3v=xn3v:
                                tt.matmul(
                                    out=view,
                                    lhsT=wsb[wname][:, D * j + 128 * i:
                                                    D * j + 128 * (i + 1)],
                                    rhs=xn3v[:, j, col0 + s0:col0 + s0 + sl],
                                    start=(j == 0), stop=(j == 7)),
                                (dw + xn_deps + ev_w[wname])
                                if (j == 0 and si == 0) else [])
                    for si, (s0, sl) in enumerate(stripes):
                        view = pyf[:, 512 * si:512 * si + sl]
                        bk_ = 6 + si
                        bank[bk_].writers = [lastT]
                        if i % 2 == 0:
                            ee = sop(lambda engs, s0=s0, sl=sl, view=view,
                                     i=i, bias_s=bias_s, scale=scale, qkt=qkt:
                                     engs.activation(
                                         out=qkt[:, s0:s0 + sl],
                                         in_=view, func=AF.Identity,
                                         bias=bias_s[:, i:i + 1], scale=scale),
                                     [lastT, d_bq, d_bk] + wart)
                        else:
                            ee = vop(lambda v, s0=s0, sl=sl, view=view,
                                     i=i, bias_s=bias_s, scale=scale, qkt=qkt:
                                     v.tensor_scalar(
                                         out=qkt[:, s0:s0 + sl], in0=view,
                                         scalar1=scale,
                                         scalar2=bias_s[:, i:i + 1],
                                         op0=ALU.mult, op1=ALU.add),
                                     [lastT, d_bq, d_bk, i_const] + wart,
                                     strict=True)
                        bank[bk_].readers.append(ee)
                        evs_i.append(ee)
                    qsem = "QK" if i % 2 == 0 else "QK2"
                    d_even = dma_s(dst64[0:64, ncols * (2 * i):
                                         ncols * (2 * i + 1)],
                                   qkt[0:64, 0:ncols],
                                   evs_i + war, sem=qsem)
                    d_odd = dma_s(dst64[0:64, ncols * (2 * i + 1):
                                        ncols * (2 * i + 2)],
                                  qkt[64:128, 0:ncols],
                                  evs_i, sem=qsem)
                    r_qktmp[i % 2].writers = evs_i
                    r_qktmp[i % 2].readers = [d_even, d_odd]
                    res.writers += [d_even, d_odd]
                for rr in r_xn_fm:
                    rr.readers.append(lastT)

            thunks.append(lambda: th_qk(q64, "q", HB, TILE, bq_s, C_QS,
                                        r_q, (0, 1, 2, 3)))
            thunks.append(lambda: th_qk(k64, "k", 0, HTOK, bk_s, 1.0,
                                        r_k, (0, 1, 2, 3)))
            thunks.append(lambda: th_qk(q64, "q", HB, TILE, bq_s, C_QS,
                                        r_q, (4, 5, 6, 7)))
            thunks.append(lambda: th_qk(k64, "k", 0, HTOK, bk_s, 1.0,
                                        r_k, (4, 5, 6, 7)))
            return thunks

        def emit_qk(c, w, pg):
            bks = BANKW[w]
            pwv = pw[w]
            lastQK = None
            for hh in range(WH):
                h = w * WH + hh
                j0 = WIN * hh
                segs = []
                s0 = j0
                while s0 < j0 + WIN:
                    s1 = min(j0 + WIN, (s0 // 512 + 1) * 512)
                    segs.append((s0, s1))
                    s0 = s1
                for (s0, s1) in segs:
                    bno = bks[s0 // 512]
                    deps = bank[bno].write() + \
                        (r_q.read() + r_k.read() if lastQK is None else [])
                    lastQK = top(
                        lambda tt, h=h, s0=s0, s1=s1, c=c, pwv=pwv, j0=j0:
                        tt.matmul(
                            out=pwv[:, s0:s1],
                            lhsT=q64[0:64, TILE * h + 128 * c:
                                     TILE * h + 128 * (c + 1)],
                            rhs=k64[0:64,
                                    HTOK * h + 128 * c + (s0 - j0):
                                    HTOK * h + 128 * c + (s1 - j0)],
                            start=True, stop=True), deps)
                    bank[bno].writers.append(lastQK)
            r_q.readers.append(lastQK)
            r_k.readers.append(lastQK)
            return lastQK

        def emit_sm1(c, w, lastQK, left_edge, right_edge):
            pwv = pw[w]
            bks = BANKW[w]
            i_mx = vop(lambda v, pwv=pwv, w=w:
                       v.reduce_max(out=negm[:, w:w + 1],
                                    in_=pwv[:, 0:1536],
                                    axis=AX.X, negate=True),
                       [lastQK] + r_negm[w].write())
            r_negm[w].writers = [i_mx]
            for b_ in bks:
                bank[b_].readers.append(i_mx)
            return [i_mx]

        def emit_exp(c, w, i_sms):
            pwv = pw[w]
            bks = BANKW[w]
            i_exp = sop(lambda engs, pwv=pwv, w=w:
                        engs.activation(out=attn_t[w][:],
                                        in_=pwv[:, 0:1536],
                                        func=AF.Exp,
                                        bias=negm[:, w:w + 1],
                                        scale=1.0),
                        i_sms + r_attn[w].write())
            r_attn[w].writers = [i_exp]
            r_negm[w].readers.append(i_exp)
            for b_ in bks:
                bank[b_].readers.append(i_exp)
            return i_exp

        def emit_av(c, w, i_exp, pg, left_edge, right_edge):
            pwv = pw[w]
            bks = BANKW[w]
            vtk = v_tok[pg]
            mv = 1 if (left_edge and c == 0) else \
                (2 if (right_edge and c == CH - 1) else 0)
            i_msk = vop(lambda v, w=w, mv=mv:
                        v.tensor_tensor(
                            out=attn_t[w][:].rearrange(
                                "p (h j) -> p h j", h=WH),
                            in0=attn_t[w][:].rearrange(
                                "p (h j) -> p h j", h=WH),
                            in1=masks[:, WIN * mv:WIN * (mv + 1)]
                            .unsqueeze(1).broadcast_to([128, WH, WIN]),
                            op=ALU.mult), [i_exp, d_mask])
            r_attn[w].writers.append(i_msk)
            i_zs = vop(lambda v, w=w:
                       v.reduce_sum(out=zsum[:, 8 * w:8 * w + 8],
                                    in_=attn_t[w][:].rearrange(
                                        "p (h j) -> p h j", h=WH),
                                    axis=AX.X), [i_msk], strict=True)
            i_eps = vop(lambda v, w=w:
                        v.tensor_scalar(out=zsum[:, 8 * w:8 * w + 8],
                                        in0=zsum[:, 8 * w:8 * w + 8],
                                        scalar1=EPS_Z, scalar2=None,
                                        op0=ALU.add), [i_zs], strict=True)
            i_rc = vop(lambda v, w=w:
                       v.reciprocal(out=rcp[:, 8 * w:8 * w + 8],
                                    in_=zsum[:, 8 * w:8 * w + 8]),
                       [i_eps], strict=True)
            i_nrm = []
            for hh in range(WH):
                i_nrm.append(
                    vop(lambda v, hh=hh, w=w:
                        v.tensor_scalar(
                            out=attn_t[w][:, WIN * hh:WIN * (hh + 1)],
                            in0=attn_t[w][:, WIN * hh:WIN * (hh + 1)],
                            scalar1=rcp[:, 8 * w + hh:8 * w + hh + 1],
                            scalar2=None, op0=ALU.mult),
                        [i_rc], strict=True))
            r_attn[w].writers += i_nrm

            t1v = pwv[:, 0:512].bitcast(BF16)
            t2v = pwv[:, 512:1024].bitcast(BF16)
            dwT = bank[bks[0]].write() + bank[bks[1]].write()
            lastTT = None
            for hh in range(WH):
                lastTT = top(
                    lambda tt, hh=hh, t1v=t1v, w=w:
                    tt.transpose(out=t1v[:, 128 * hh:128 * (hh + 1)],
                                 in_=attn_t[w][:, WIN * hh:WIN * hh + 128],
                                 identity=idb[:]),
                    (i_nrm + dwT) if hh == 0 else [])
                lastTT = top(
                    lambda tt, hh=hh, t2v=t2v, w=w:
                    tt.transpose(out=t2v[0:64, 128 * hh:128 * (hh + 1)],
                                 in_=attn_t[w][:, WIN * hh + 128:
                                                 WIN * hh + 192],
                                 identity=idb[:]), [])
            bank[bks[0]].writers = [lastTT]
            bank[bks[1]].writers = [lastTT]
            r_attn[w].readers.append(lastTT)

            i_e1 = vop(lambda v, w=w, t1v=t1v:
                       v.tensor_copy(out=at1[w][:], in_=t1v[:, 0:1024]),
                       [lastTT] + r_at[w].write())
            i_e2 = sop(lambda engs, w=w, t2v=t2v:
                       engs.activation(out=at2[w][:],
                                       in_=t2v[0:64, 0:1024],
                                       func=AF.Copy, bias=0.0,
                                       scale=1.0),
                       [lastTT])
            r_at[w].writers = [i_e1, i_e2]
            bank[bks[0]].readers.append(i_e1)
            bank[bks[1]].readers.append(i_e2)

            cview = pwv[:, 1024:1536]
            dwC = bank[bks[2]].write()
            lastAV = None
            for hh in range(WH):
                h = w * WH + hh
                deps = []
                if hh == 0:
                    deps = dwC + [i_e1, i_e2] + r_v[pg][c].read() \
                        + r_v[pg][c + 1].read()
                lastAV = top(
                    lambda tt, hh=hh, h=h, c=c, w=w, cview=cview, vtk=vtk:
                    tt.matmul(
                        out=cview[64 * (hh % 2):64 * (hh % 2) + 64,
                                  128 * (hh // 2):128 * (hh // 2) + 128],
                        lhsT=vtk[:, D * c + 64 * h:D * c + 64 * (h + 1)],
                        rhs=at1[w][:, 128 * hh:128 * (hh + 1)],
                        start=True, stop=False), deps)
                lastAV = top(
                    lambda tt, hh=hh, h=h, c=c, w=w, cview=cview, vtk=vtk:
                    tt.matmul(
                        out=cview[64 * (hh % 2):64 * (hh % 2) + 64,
                                  128 * (hh // 2):128 * (hh // 2) + 128],
                        lhsT=vtk[0:64, D * (c + 1) + 64 * h:
                                 D * (c + 1) + 64 * (h + 1)],
                        rhs=at2[w][:, 128 * hh:128 * (hh + 1)],
                        start=False, stop=True), [])
            bank[bks[2]].writers = [lastAV]
            r_at[w].readers.append(lastAV)
            r_v[pg][c].readers.append(lastAV)
            r_v[pg][c + 1].readers.append(lastAV)

            i_ce = sop(lambda engs, w=w, cview=cview, c=c:
                       engs.activation(out=ctx[:, 512 * w:512 * (w + 1)],
                                       in_=cview, func=AF.Copy,
                                       bias=0.0, scale=1.0),
                       [lastAV] + (r_ctx.write() if w == 0 else []))
            r_ctx.writers.append(i_ce)
            bank[bks[2]].readers.append(i_ce)
            return i_ce

        def ln_post(src_view, src_dep, bank_ids, dst_bf, r_dst, col, r_st):
            i_cp = sop(lambda engs: engs.activation(
                out=dst_bf[:], in_=src_view, func=AF.Copy, bias=0.0,
                scale=1.0, accum_out=sumx[:, col:col + 1]),
                [src_dep] + r_dst.write() + r_st.read())
            r_dst.writers = [i_cp]
            i_sq2 = sop(lambda engs: engs.activation(
                out=scrap[:], in_=dst_bf[:], func=AF.Square, bias=0.0,
                scale=1.0, accum_out=sumq[:, col:col + 1]),
                [i_cp], strict=True)
            r_dst.readers.append(i_sq2)
            for b_ in bank_ids:
                bank[b_].readers.append(i_cp)
            sdep = [i_cp, i_sq2] + r_st.write()
            i_m2 = vop(lambda v: v.tensor_scalar(
                out=mst[:, col:col + 1], in0=sumx[:, col:col + 1],
                scalar1=1.0 / D, scalar2=None, op0=ALU.mult), sdep)
            i_mm2 = vop(lambda v: v.tensor_tensor(
                out=msq[:, col:col + 1], in0=mst[:, col:col + 1],
                in1=mst[:, col:col + 1], op=ALU.mult), [i_m2], strict=True)
            i_v2 = vop(lambda v: v.scalar_tensor_tensor(
                out=vst[:, col:col + 1], in0=sumq[:, col:col + 1],
                scalar=1.0 / D, in1=msq[:, col:col + 1],
                op0=ALU.mult, op1=ALU.subtract), [i_mm2], strict=True)
            i_l2 = sop(lambda engs: engs.activation(
                out=vst[:, col:col + 1], in_=vst[:, col:col + 1],
                func=AF.Ln, bias=C_EPS, scale=1.0), [i_v2])
            i_r2x = sop(lambda engs: engs.activation(
                out=rst[:, col:col + 1], in_=vst[:, col:col + 1],
                func=AF.Exp, bias=0.0, scale=C_NHALF),
                [i_l2], strict=True)
            r_st.writers = [i_m2, i_mm2, i_v2, i_l2, i_r2x]
            return i_cp, i_r2x

        def emit_d1(t, c, ctx_done, a):
            pg = t % 2
            xbf = x_bf
            tb = tbf[c % 2]
            x3f = xn3_f[c % 2]
            r_tb = r_tbf[c % 2]
            r_x3 = r_xn3f[c % 2]
            yv = pyf[:, 0:1024]
            dwY = bank[6].write() + bank[7].write()
            lastWO = None
            for pch in range(8):
                for h2 in range(2):
                    deps = (dwY + ctx_done + ev_w["o"]) \
                        if (pch == 0 and h2 == 0) else []
                    lastWO = top(
                        lambda tt, pch=pch, h2=h2, yv=yv, c=c:
                        tt.matmul(
                            out=yv[:, 512 * h2:512 * (h2 + 1)],
                            lhsT=ctx[:, 128 * pch:128 * (pch + 1)],
                            rhs=wsb["o"][:, D * pch + 512 * h2:
                                         D * pch + 512 * (h2 + 1)],
                            start=(pch == 0), stop=False), deps)
            for h2 in range(2):
                lastWO = top(lambda tt, h2=h2, yv=yv:
                             tt.matmul(out=yv[:, 512 * h2:512 * (h2 + 1)],
                                       lhsT=ones_bf[:],
                                       rhs=bo_bf[:, 512 * h2:
                                                 512 * (h2 + 1)],
                                       start=False, stop=True),
                             [i_r2] if (t == 0 and c == 0) else [])
            bank[6].writers = [lastWO]
            bank[7].writers = [lastWO]
            r_ctx.readers.append(lastWO)

            i_ycp, i_yrs = ln_post(yv, lastWO, [6, 7], tb, r_tb, 10, r_st2)
            i_t1 = vop(lambda v, tb=tb: v.tensor_scalar(
                out=tb[:], in0=tb[:], scalar1=mst[:, 10:11],
                scalar2=rst[:, 10:11], op0=ALU.subtract, op1=ALU.mult),
                [i_yrs, i_ycp], strict=True)
            r_tb.writers.append(i_t1)
            r_st2.readers.append(i_t1)

            i_x2 = vop(lambda v, c=c, xbf=xbf, tb=tb: v.tensor_tensor(
                out=xbf[:, D * c:D * (c + 1)],
                in0=xbf[:, D * c:D * (c + 1)], in1=tb[:], op=ALU.add),
                [i_t1] + r_xbf[c].write())
            r_xbf[c].writers = [i_x2]
            r_tb.readers.append(i_x2)

            i_s3 = vop(lambda v, c=c, xbf=xbf: v.reduce_sum(
                out=sumx[:, 11:12], in_=xbf[:, D * c:D * (c + 1)],
                axis=AX.X), [i_x2] + r_st3.read())
            i_q3 = sop(lambda engs, c=c, xbf=xbf: engs.activation(
                out=scrap[:], in_=xbf[:, D * c:D * (c + 1)],
                func=AF.Square, bias=0.0, scale=1.0,
                accum_out=sumq[:, 11:12]), [i_x2])
            r_xbf[c].readers += [i_s3, i_q3]
            sdep = [i_s3, i_q3] + r_st3.write()
            i_m3 = vop(lambda v: v.tensor_scalar(
                out=mst[:, 11:12], in0=sumx[:, 11:12], scalar1=1.0 / D,
                scalar2=None, op0=ALU.mult), sdep)
            i_mm3 = vop(lambda v: v.tensor_tensor(
                out=msq[:, 11:12], in0=mst[:, 11:12], in1=mst[:, 11:12],
                op=ALU.mult), [i_m3], strict=True)
            i_v3 = vop(lambda v: v.scalar_tensor_tensor(
                out=vst[:, 11:12], in0=sumq[:, 11:12], scalar=1.0 / D,
                in1=msq[:, 11:12], op0=ALU.mult, op1=ALU.subtract),
                [i_mm3], strict=True)
            i_l3 = sop(lambda engs: engs.activation(
                out=vst[:, 11:12], in_=vst[:, 11:12], func=AF.Ln,
                bias=C_EPS, scale=1.0), [i_v3])
            i_r3x = sop(lambda engs: engs.activation(
                out=rst[:, 11:12], in_=vst[:, 11:12], func=AF.Exp,
                bias=0.0, scale=C_NHALF), [i_l3], strict=True)
            r_st3.writers = [i_m3, i_mm3, i_v3, i_l3, i_r3x]

            i_a3 = vop(lambda v, c=c, xbf=xbf: v.tensor_scalar(
                out=xn3_t[:], in0=xbf[:, D * c:D * (c + 1)],
                scalar1=mst[:, 11:12], scalar2=rst[:, 11:12],
                op0=ALU.subtract, op1=ALU.mult),
                [i_r3x] + r_xn3t.write(), strict=True)
            r_xn3t.writers = [i_a3]
            r_xbf[c].readers.append(i_a3)
            r_st3.readers.append(i_a3)

            return i_a3

        def emit_d1b(t, c, i_a3):
            x3f = xn3_f[c % 2]
            r_x3 = r_xn3f[c % 2]
            dw6 = bank[6].write()
            lastT3 = None
            for dch in range(8):
                lastT3 = top(lambda tt, dch=dch:
                             tt.transpose(
                                 out=pxnt[:, 128 * dch:128 * (dch + 1)],
                                 in_=xn3_t[:, 128 * dch:128 * (dch + 1)],
                                 identity=idb[:]),
                             ([i_a3] + dw6) if dch == 0 else [])
            bank[6].writers = [lastT3]
            r_xn3t.readers.append(lastT3)
            i_e3 = vop(lambda v, x3f=x3f: v.tensor_copy(
                out=x3f[:].rearrange("p (c n) -> p c n", c=8),
                in_=pxnt[:].rearrange("p (c z) -> p c z", c=8)),
                [lastT3] + r_x3.write())
            r_x3.writers = [i_e3]
            bank[6].readers.append(i_e3)
            return i_e3

        def emit_d2(t, c, a):
            pg = t % 2
            xbf = x_bf
            tb = tbf[c % 2]
            x3f = xn3_f[c % 2]
            r_tb = r_tbf[c % 2]
            r_x3 = r_xn3f[c % 2]
            i_e3 = r_x3.writers[0] if r_x3.writers else None
            fv = pyf[:, 0:1024]
            dwF = bank[6].write() + bank[7].write()
            lastF = None
            for pch in range(8):
                for h2 in range(2):
                    deps = (dwF + ([i_e3] if i_e3 is not None else []) +
                            (ev_m if (t == 0 and c == 0) else [])) \
                        if (pch == 0 and h2 == 0) else []
                    lastF = top(
                        lambda tt, pch=pch, h2=h2, fv=fv, x3f=x3f:
                        tt.matmul(
                            out=fv[:, 512 * h2:512 * (h2 + 1)],
                            lhsT=x3f[:, 128 * pch:128 * (pch + 1)],
                            rhs=m_sb[:, D * pch + 512 * h2:
                                     D * pch + 512 * (h2 + 1)],
                            start=(pch == 0), stop=False), deps)
            for h2 in range(2):
                lastF = top(lambda tt, h2=h2, fv=fv:
                            tt.matmul(out=fv[:, 512 * h2:512 * (h2 + 1)],
                                      lhsT=ones_bf[:],
                                      rhs=bf_bf[:, 512 * h2:
                                                512 * (h2 + 1)],
                                      start=False, stop=True),
                            [i_r3] if (t == 0 and c == 0) else [])
            bank[6].writers = [lastF]
            bank[7].writers = [lastF]
            r_x3.readers.append(lastF)

            i_fcp, i_frs = ln_post(fv, lastF, [6, 7], tb, r_tb, 12, r_st4)
            i_t2 = vop(lambda v, tb=tb: v.tensor_scalar(
                out=tb[:], in0=tb[:], scalar1=mst[:, 12:13],
                scalar2=rst[:, 12:13], op0=ALU.subtract, op1=ALU.mult),
                [i_frs, i_fcp], strict=True)
            r_tb.writers.append(i_t2)
            r_st4.readers.append(i_t2)

            i_of = vop(lambda v, c=c, xbf=xbf, tb=tb: v.tensor_tensor(
                out=outsb[:], in0=xbf[:, D * c:D * (c + 1)], in1=tb[:],
                op=ALU.add), [i_t2] + r_out.write())
            r_out.writers = [i_of]
            r_tb.readers.append(i_of)
            r_xbf[c].readers.append(i_of)

            dd = dma_g(y_d[a + 128 * c:a + 128 * (c + 1), :], outsb[:],
                       [i_of])
            r_out.readers.append(dd)
            return lastF

        SCHED = {0: [0, 1], 1: [5, 6, 2, 3, 14, 15],
                 2: [7, 8, 4, 16, 17, 10],
                 3: [9, 18, 19, 20, 21, 22, 11]}
        p0 = make_preps(0)
        for ix in (0, 1, 5, 2, 6, 3, 7, 4, 8, 9, 10, 11, 12, 13,
                   14, 15, 16, 17, 18, 19, 20, 21, 22):
            p0[ix]()
        pend1 = None
        pend2 = None
        carry = None
        pre = None
        for t in range(NTILE):
            a = t * TILE
            seq_lo = (a // S) * S
            left_edge = (a == seq_lo)
            right_edge = (a + TILE == seq_lo + S)
            nxt = make_preps(t + 1) if t + 1 < NTILE else None
            for c in range(CH):
                qkA = emit_qk(c, 0, t % 2)
                qkB = emit_qk(c, 1, t % 2)
                smA = emit_sm1(c, 0, qkA, left_edge, right_edge)
                smB = emit_sm1(c, 1, qkB, left_edge, right_edge)
                expA = emit_exp(c, 0, smA)
                expB = emit_exp(c, 1, smB)
                if pend2 is not None:
                    emit_d2(pend2[0], pend2[1], pend2[3])
                    pend2 = None
                pend_b = None
                if pend1 is not None:
                    i_a3_ = emit_d1(*pend1)
                    pend_b = (pend1[0], pend1[1], i_a3_)
                    pend2 = pend1
                    pend1 = None
                ceA = emit_av(c, 0, expA, t % 2, left_edge, right_edge)
                ceB = emit_av(c, 1, expB, t % 2, left_edge, right_edge)
                pend1 = (t, c, [ceA, ceB], a)
                if c == 0 and carry is not None:
                    carry[0]()
                if c == 1 and carry is not None:
                    carry[1]()
                    carry = None
                if nxt is not None:
                    for ix in SCHED[c]:
                        nxt[ix]()
                if pend_b is not None:
                    emit_d1b(*pend_b)
            if nxt is not None:
                carry = (nxt[12], nxt[13])
            MARKS["t%d_end" % t] = len(tr.ops)
        emit_d2(pend2[0], pend2[1], pend2[3])
        i_a3_ = emit_d1(*pend1)
        emit_d1b(pend1[0], pend1[1], i_a3_)
        emit_d2(pend1[0], pend1[1], pend1[3])

        kstop = os.environ.get("KSTOP")
        if kstop:
            nmax = MARKS[kstop] if kstop in MARKS else int(kstop)
            tr.ops = tr.ops[:nmax]
            print("KSTOP at", nmax, "ops")

        with (nc.semaphore("semT") as semT, nc.semaphore("semV") as semV,
              nc.semaphore("semS") as semS, nc.semaphore("semG") as semG,
              nc.semaphore("semIN") as semIN, nc.semaphore("semOUT") as semOUT,
              nc.semaphore("semIN2") as semIN2,
              nc.semaphore("semIN3") as semIN3,
              nc.semaphore("semIN4") as semIN4,
              nc.semaphore("semQK") as semQK,
              nc.semaphore("semQK2") as semQK2,
              nc.Block() as block):
            sems = dict(T=semT, V=semV, S=semS, G=semG, IN=semIN, OUT=semOUT,
                        IN2=semIN2, IN3=semIN3, IN4=semIN4, QK=semQK,
                        QK2=semQK2)
            inc_no = {}
            cnt = {k: 0 for k in
                   ("T", "V", "S", "G", "IN", "IN2", "IN3", "IN4",
                    "OUT", "QK", "QK2")}
            for i, o in enumerate(tr.ops):
                if o["dma"] is not None:
                    cnt[o["dma"]] += 16
                    inc_no[i] = (o["dma"], cnt[o["dma"]])
                elif i in tr.need_inc:
                    cnt[o["eng"]] += 1
                    inc_no[i] = (o["eng"], cnt[o["eng"]])
            total_out = cnt["OUT"]

            def emit(eng_name, engine):
                waited = {}
                for i, o in enumerate(tr.ops):
                    if o["eng"] != eng_name:
                        continue
                    for d in o["deps"]:
                        tok = inc_no.get(d)
                        if tok is None:
                            continue
                        sname, val = tok
                        if (sname == eng_name and tr.ops[d]["dma"] is None
                                and not o["strict"]
                                and not os.environ.get("KSTRICT")
                                and eng_name == "T"):
                            continue
                        if waited.get(sname, 0) >= val:
                            continue
                        engine.wait_ge(sems[sname], val)
                        waited[sname] = val
                    ins = o["fn"](engine)
                    tok = inc_no.get(i)
                    if tok is not None:
                        ins.then_inc(sems[tok[0]], 16 if o["dma"] else 1)
                if eng_name == "Y" and total_out:
                    engine.wait_ge(sems["OUT"], total_out)

            @block.tensor
            def _(engine):
                emit("T", engine)

            @block.vector
            def _(engine):
                emit("V", engine)

            @block.scalar
            def _(engine):
                emit("S", engine)

            @block.gpsimd
            def _(engine):
                emit("G", engine)

            @block.sync
            def _(engine):
                emit("Y", engine)

    return nc


def host_inputs(inp):
    x = np.ascontiguousarray(inp["L0_attn_x"], np.float32).reshape(B * S, D)
    pq_ = np.arange(D, dtype=np.int64)
    mm = (pq_[:, None] * pq_[None, :]) % D
    ang = (2.0 * np.pi / D) * mm
    Cp = (np.cos(ang) / 32.0).astype(np.float32).astype(ml_dtypes.bfloat16)
    Sn = (-np.sin(ang) / 32.0).astype(np.float32).astype(ml_dtypes.bfloat16)

    j = np.arange(WIN)[None, :]
    q = np.arange(128)[:, None]
    band = (j - 32 - q >= -32) & (j - 32 - q <= 32)
    mid = np.where(band, 1.0, 0.0).astype(np.float32)
    left = np.where(band & (j >= 32), 1.0, 0.0).astype(np.float32)
    right = np.where(band & (j < 160), 1.0, 0.0).astype(np.float32)
    masks = np.concatenate([mid, left, right],
                           axis=1).astype(ml_dtypes.bfloat16)

    rows = np.zeros((65, D), np.float32)
    rows[0] = inp["L0_attn_bv"]
    rows[32] = inp["L0_attn_bo"]
    rows[64] = inp["L0_ffn_bias"]

    base = {
        "wq": np.ascontiguousarray(inp["L0_attn_Wq"].T, np.float32),
        "wk": np.ascontiguousarray(inp["L0_attn_Wk"].T, np.float32),
        "wv": np.ascontiguousarray(inp["L0_attn_Wv"].T, np.float32),
        "wo": np.ascontiguousarray(inp["L0_attn_Wo"].T, np.float32),
        "bq": np.ascontiguousarray(
            (inp["L0_attn_bq"].astype(np.float32) * 0.125).reshape(8, 128).T),
        "bk": np.ascontiguousarray(
            inp["L0_attn_bk"].astype(np.float32).reshape(8, 128).T),
        "rows": rows,
        "Cp": Cp,
        "Sn": Sn,
        "wrT": np.ascontiguousarray(inp["L0_ffn_w_freq_real"].T, np.float32),
        "wiT": np.ascontiguousarray(inp["L0_ffn_w_freq_imag"].T, np.float32),
        "masks": masks,
        "idb": np.eye(128).astype(ml_dtypes.bfloat16),
    }
    maps = []
    for c in range(NCORES):
        m = dict(base)
        m["x"] = np.ascontiguousarray(x[c * TOK:(c + 1) * TOK])
        maps.append(m)
    return maps


def kernel(**inputs):
    maps = host_inputs(inputs)
    nc = build_program()
    res = run_bass_kernel_spmd(nc, maps, core_ids=list(range(NCORES)))
    out = np.concatenate([np.asarray(r["y"], np.float32)
                          for r in res.results], axis=0)
    return out.reshape(B, S, D)


# version 31 (final)
# speedup vs baseline: 1.0125x; 1.0125x over previous
import os
import numpy as np
import ml_dtypes
from contextlib import ExitStack

import concourse.bass as bass
import concourse.mybir as mybir
import concourse.bass_utils as _bu
from concourse.bass_utils import run_bass_kernel_spmd

if not getattr(_bu, "_ldwopt_patched", False):
    _orig_run_command = _bu.run_command

    def _run_command_ldwopt(argv, **kw):
        argv = ["--enable-ldw-opt=true" if a == "--enable-ldw-opt=false"
                else a for a in argv]
        return _orig_run_command(argv, **kw)

    _bu.run_command = _run_command_ldwopt
    _bu._ldwopt_patched = True

F32 = mybir.dt.float32
BF16 = mybir.dt.bfloat16
AF = mybir.ActivationFunctionType
ALU = mybir.AluOpType
AX = mybir.AxisListType

D = 1024
H = 16
S = 2048
B = 32
NCORES = 8
TOK = (B // NCORES) * S
TILE = 512
NTILE = TOK // TILE
HB = 32
HCH = 5
HTOK = TILE + 2 * HB
CH = TILE // 128
WIN = 192
NEG = -1.0e30
EPS_LN = 1.0e-5
EPS_Z = 1.0e-30
WH = 8

MARKS = {}


class Trace:
    def __init__(self):
        self.ops = []
        self.need_inc = set()

    def op(self, eng, fn, deps=(), dma_sem=None, strict=False):
        idx = len(self.ops)
        deps = [d for d in deps if d is not None]
        self.ops.append(dict(eng=eng, fn=fn, deps=deps, dma=dma_sem,
                             strict=strict))
        for d in deps:
            self.need_inc.add(d)
        return idx


class Res:
    def __init__(self):
        self.writers = []
        self.readers = []

    def write(self):
        deps = self.writers + self.readers
        self.writers = []
        self.readers = []
        return deps

    def read(self):
        return list(self.writers)


def build_program():
    nc = bass.Bass("TRN2", target_bir_lowering=False,
                   detect_race_conditions=False,
                   dynamic_dma_scratch_size=12288)

    def din(name, shape, dt=F32):
        return nc.dram_tensor(name, shape, dt, kind="ExternalInput").ap()

    x_d = din("x", [TOK, D])
    y_d = nc.dram_tensor("y", [TOK, D], F32, kind="ExternalOutput").ap()
    w_d = {n: din("w" + n, [D, D]) for n in ("q", "k", "v", "o")}
    bq_d = din("bq", [128, 8])
    bk_d = din("bk", [128, 8])
    rows_d = din("rows", [65, D])
    cp_d = din("Cp", [D, D], BF16)
    sn_d = din("Sn", [D, D], BF16)
    wrT_d = din("wrT", [D, D])
    wiT_d = din("wiT", [D, D])
    masks_d = din("masks", [128, 3 * WIN], BF16)
    idb_d = din("idb", [128, 128], BF16)

    tr = Trace()

    with ExitStack() as st:
        e = st.enter_context

        wsb = {n: e(nc.sbuf_tensor("w_" + n, [128, 8 * D], BF16))
               for n in ("q", "k", "v", "o")}
        m_sb = e(nc.sbuf_tensor("m_sb", [128, 8 * D], BF16))
        masks = e(nc.sbuf_tensor("masks_sb", [128, 3 * WIN], BF16))
        idb = e(nc.sbuf_tensor("idb_sb", [128, 128], BF16))
        bq_s = e(nc.sbuf_tensor("bq_s", [128, 8], F32))
        bk_s = e(nc.sbuf_tensor("bk_s", [128, 8], F32))
        bv_bf = e(nc.sbuf_tensor("bv_bf", [1, D], BF16))
        bo_bf = e(nc.sbuf_tensor("bo_bf", [1, D], BF16))
        bf_bf = e(nc.sbuf_tensor("bf_bf", [1, D], BF16))
        ones_bf = e(nc.sbuf_tensor("ones_bf", [1, 128], BF16))
        kconst = e(nc.sbuf_tensor("kconst", [128, 4], F32))
        sumx = e(nc.sbuf_tensor("sumx", [128, 16], F32))
        sumq = e(nc.sbuf_tensor("sumq", [128, 16], F32))
        mst = e(nc.sbuf_tensor("mst", [128, 16], F32))
        msq = e(nc.sbuf_tensor("msq", [128, 16], F32))
        vst = e(nc.sbuf_tensor("vst", [128, 16], F32))
        rst = e(nc.sbuf_tensor("rst", [128, 16], F32))
        negm = e(nc.sbuf_tensor("negm", [128, 16], F32))
        zsum = e(nc.sbuf_tensor("zsum", [128, 16], F32))
        rcp = e(nc.sbuf_tensor("rcp", [128, 16], F32))

        pw = [e(nc.psum_tensor("pw%d" % w, [128, 3 * 512], F32))
              for w in range(2)]
        pyf = e(nc.psum_tensor("pyf", [128, 1024], F32))
        pxnt = pyf[:, 0:512].bitcast(BF16)
        bank = [Res() for _ in range(8)]
        BANKW = [[0, 1, 2], [3, 4, 5]]

        mb = ExitStack()
        cb = mb.enter_context(nc.sbuf_tensor("cb", [128, 8 * D], BF16))
        snb = mb.enter_context(nc.sbuf_tensor("snb", [128, 8 * D], BF16))
        wrb = mb.enter_context(nc.sbuf_tensor("wrb", [128, 8 * D], BF16))
        wib = mb.enter_context(nc.sbuf_tensor("wib", [128, 8 * D], BF16))
        stg_mb = [mb.enter_context(nc.sbuf_tensor("stgmb%d" % i, [128, D], F32))
                  for i in range(2)]
        rows_f = mb.enter_context(nc.sbuf_tensor("rows_f", [65, D], F32))
        r_stg_mb = [Res(), Res()]

        nIN = [0]
        nOUT = [0]

        def dma_in(dst_ap, src_ap, deps, sem="IN"):
            def fn(eng, dst=dst_ap, src=src_ap):
                return eng.dma_start(out=dst, in_=src)
            idx = tr.op("Y", fn, deps, dma_sem=sem)
            nIN[0] += 1
            return idx

        def dma_g(dst_ap, src_ap, deps):
            def fn(eng, dst=dst_ap, src=src_ap):
                return eng.dma_start(out=dst, in_=src)
            idx = tr.op("Y", fn, deps, dma_sem="OUT")
            nOUT[0] += 1
            return idx

        def dma_s(dst_ap, src_ap, deps, sem="QK"):
            def fn(eng, dst=dst_ap, src=src_ap):
                return eng.dma_start(out=dst, in_=src)
            return tr.op("Y", fn, deps, dma_sem=sem)

        def vop(fn, deps, strict=False):
            return tr.op("V", fn, deps, strict=strict)

        def sop(fn, deps, strict=False):
            return tr.op("S", fn, deps, strict=strict)

        def top(fn, deps):
            return tr.op("T", fn, deps)

        d_mask = dma_in(masks[:], masks_d, [])
        d_idb = dma_in(idb[:], idb_d, [])
        d_bq = dma_in(bq_s[:], bq_d, [])
        d_bk = dma_in(bk_s[:], bk_d, [])
        d_rows = dma_in(rows_f[:], rows_d, [])

        i_ones = vop(lambda v: v.memset(ones_bf[:], 1.0), [])
        vop(lambda v: v.memset(kconst[:, 0:1], EPS_LN), [])
        vop(lambda v: v.memset(kconst[:, 1:2], -0.5), [])
        vop(lambda v: v.memset(kconst[:, 2:3], 0.125), [])
        i_const = vop(lambda v: v.memset(kconst[:, 3:4], -1.0), [])
        C_EPS = kconst[:, 0:1]
        C_NHALF = kconst[:, 1:2]
        C_QS = kconst[:, 2:3]
        C_NEG1 = kconst[:, 3:4]
        i_r1 = vop(lambda v: v.tensor_copy(out=bv_bf[:], in_=rows_f[0:1, :]),
                   [d_rows])
        i_r2 = vop(lambda v: v.tensor_copy(out=bo_bf[:], in_=rows_f[32:33, :]),
                   [d_rows])
        i_r3 = vop(lambda v: v.tensor_copy(out=bf_bf[:], in_=rows_f[64:65, :]),
                   [d_rows])

        def load_bf(dst_sb, src_d, extra_dst=None, gate=()):
            evs = []
            for j in range(8):
                p = j % 2
                dd = dma_in(stg_mb[p][:], src_d[128 * j:128 * (j + 1), :],
                            r_stg_mb[p].write() + list(gate),
                            sem=("IN" if p == 0 else "IN2"))
                r_stg_mb[p].writers = [dd]

                def fn(engs, j=j, p=p, dst=dst_sb):
                    return engs.activation(out=dst[:, D * j:D * (j + 1)],
                                           in_=stg_mb[p][:], func=AF.Copy,
                                           bias=0.0, scale=1.0)
                cc = sop(fn, [dd])
                r_stg_mb[p].readers.append(cc)
                evs.append(cc)
                if extra_dst is not None:
                    def fn2(engs, j=j, p=p, dst=extra_dst):
                        return engs.activation(out=dst[:, D * j:D * (j + 1)],
                                               in_=stg_mb[p][:],
                                               func=AF.Identity,
                                               bias=0.0, scale=C_NEG1)
                    c2 = sop(fn2, [dd, i_const])
                    r_stg_mb[p].readers.append(c2)
                    evs.append(c2)
            return evs

        ev_cb = [dma_in(cb[:].rearrange("p (j n) -> p j n", j=8),
                        cp_d[:].rearrange("(j p) n -> p j n", p=128), [],
                        sem="IN")]
        ev_snb = [dma_in(snb[:].rearrange("p (j n) -> p j n", j=8),
                         sn_d[:].rearrange("(j p) n -> p j n", p=128), [],
                         sem="IN2")]
        ev_wrb = load_bf(wrb, wrT_d)
        ev_wib = load_bf(wib, wiT_d, extra_dst=wsb["v"])

        def mm_product(dst_sb, lhs_a, rhs_a, lhs_b, rhs_b, deps_in):
            evs = []
            for i in range(8):
                dw = bank[6].write() + bank[7].write()
                last = None
                for ci, (lhs, rhs) in enumerate(((lhs_a, rhs_a),
                                                 (lhs_b, rhs_b))):
                    for j in range(8):
                        for hf in range(2):
                            def fmm(t, lhs=lhs, rhs=rhs, i=i, j=j, hf=hf,
                                    st_=(ci == 0 and j == 0),
                                    sp_=(ci == 1 and j == 7)):
                                return t.matmul(
                                    out=pyf[:, 512 * hf:512 * (hf + 1)],
                                    lhsT=lhs[:, D * j + 128 * i:
                                             D * j + 128 * (i + 1)],
                                    rhs=rhs[:, D * j + 512 * hf:
                                            D * j + 512 * (hf + 1)],
                                    start=st_, stop=sp_)
                            last = top(fmm, (dw + deps_in)
                                       if (ci == 0 and j == 0 and hf == 0)
                                       else [])
                bank[6].writers = [last]
                bank[7].writers = [last]

                def fev(engs, i=i, dst=dst_sb):
                    return engs.activation(
                        out=dst[:, D * i:D * (i + 1)],
                        in_=pyf[:, 0:1024], func=AF.Copy, bias=0.0, scale=1.0)
                ee = sop(fev, [last])
                bank[6].readers.append(ee)
                bank[7].readers.append(ee)
                evs.append(ee)
            return evs

        ev_w = {}
        ev_w["o"] = load_bf(wsb["o"], w_d["o"])

        ev_p1 = mm_product(wsb["q"], wrb, cb, wib, snb,
                           ev_cb + ev_snb + ev_wrb + ev_wib)
        ev_p2 = mm_product(wsb["k"], wsb["v"], cb, wrb, snb, ev_p1)
        ev_w["v"] = load_bf(wsb["v"], w_d["v"], gate=ev_p2)
        ev_m = mm_product(m_sb, cb, wsb["q"], snb, wsb["k"], ev_p2)

        mb.close()

        stg = [e(nc.sbuf_tensor("stg%d" % i, [128, D], F32)) for i in range(4)]
        r_stg = [Res() for _ in range(4)]
        SEMS_IN = ("IN", "IN2", "IN3", "IN4")
        for n in ("q", "k"):
            evs = []
            for j in range(8):
                p = j % 4
                dd = dma_in(stg[p][:], w_d[n][128 * j:128 * (j + 1), :],
                            r_stg[p].write() + ev_m,
                            sem=SEMS_IN[p])
                r_stg[p].writers = [dd]

                def fn(engs, j=j, p=p, dst=wsb[n]):
                    return engs.activation(out=dst[:, D * j:D * (j + 1)],
                                           in_=stg[p][:], func=AF.Copy,
                                           bias=0.0, scale=1.0)
                cc = sop(fn, [dd])
                r_stg[p].readers.append(cc)
                evs.append(cc)
            ev_w[n] = evs
        MARKS["mbuild"] = len(tr.ops)

        xn_fm = e(nc.sbuf_tensor("xn_fm", [128, 8 * HTOK], BF16))
        q64 = e(nc.sbuf_tensor("q64", [64, H * TILE], BF16))
        k64 = e(nc.sbuf_tensor("k64", [64, H * HTOK], BF16))
        v_tok = [e(nc.sbuf_tensor("v_tok%d" % i, [128, HCH * D], BF16))
                 for i in range(2)]
        x_bf = e(nc.sbuf_tensor("x_bf", [128, CH * D], BF16))
        xn_tok = [e(nc.sbuf_tensor("xn_tok%d" % i, [128, D], BF16))
                  for i in range(2)]
        scrap = e(nc.sbuf_tensor("scrap", [128, D], BF16))
        attn_t = [e(nc.sbuf_tensor("attn%d" % i, [128, WH * WIN], BF16))
                  for i in range(2)]
        at1_s = e(nc.sbuf_tensor("at1_s", [128, 1024], BF16))
        at2_s = e(nc.sbuf_tensor("at2_s", [64, 1024], BF16))
        at1 = [at1_s, at1_s]
        at2 = [at2_s, at2_s]
        ctx = e(nc.sbuf_tensor("ctx", [128, 1024], BF16))
        tbf = [e(nc.sbuf_tensor("tbf%d" % i, [128, 1024], BF16))
               for i in range(2)]
        qktmp = [e(nc.sbuf_tensor("qktmp%d" % i, [128, HTOK], BF16))
                 for i in range(2)]
        outsb = e(nc.sbuf_tensor("outsb", [128, D], F32))

        xn3_t = e(nc.sbuf_tensor("xn3_t", [128, D], BF16))
        xn3_f = [e(nc.sbuf_tensor("xn3_f%d" % i, [128, D], BF16))
                 for i in range(2)]

        r_stgS = r_stg
        r_xn_tok = [Res(), Res()]
        r_scrap = Res()
        r_xn3t = Res()
        r_xn3f = [Res(), Res()]
        r_xn_fm = [Res() for _ in range(HCH)]
        r_q = Res()
        r_k = Res()
        r_v = [[Res() for _ in range(HCH)] for _ in range(2)]
        r_xbf = [Res() for _ in range(CH)]
        r_attn = [Res(), Res()]
        r_at_s = Res()
        r_at = [r_at_s, r_at_s]
        r_ctx = Res()
        r_tbf = [Res(), Res()]
        r_out = Res()
        r_stats = [Res() for _ in range(HCH)]
        r_st2 = Res()
        r_st3 = Res()
        r_st4 = Res()
        r_negm = [Res(), Res()]
        r_stT = [Res(), Res()]
        r_qktmp = [Res(), Res()]

        def make_preps(t):
            pg = t % 2
            a = t * TILE
            seq_lo = (a // S) * S
            seq_hi = seq_lo + S
            xnf = xn_fm
            vtk = v_tok[pg]
            xbf = x_bf
            thunks = []

            def th_a(hc):
                p = hc % 4
                r0 = a - HB + 128 * hc
                r1 = r0 + (128 if hc < HCH - 1 else 64)
                v0, v1 = max(r0, seq_lo), min(r1, seq_hi)
                deps = r_stgS[p].write()
                dd = dma_in(stg[p][v0 - r0:v1 - r0, :], x_d[v0:v1, :], deps,
                            sem=SEMS_IN[p])
                r_stgS[p].writers = [dd]
                dm = dd
                if v0 > r0:
                    dm = vop(lambda v, p=p, n=v0 - r0:
                             v.memset(stg[p][0:n, :], 0.0), deps)
                    r_stgS[p].writers.append(dm)
                if v1 < r1:
                    dm = vop(lambda v, p=p, n0=v1 - r0, n1=r1 - r0:
                             v.memset(stg[p][n0:n1, :], 0.0), deps)
                    r_stgS[p].writers.append(dm)

                i_sum = vop(lambda v, p=p, hc=hc:
                            v.reduce_sum(out=sumx[:, hc:hc + 1], in_=stg[p][:],
                                         axis=AX.X),
                            [dd, dm] + r_stats[hc].read())

                def f_sq(engs, p=p, hc=hc):
                    return engs.activation(out=scrap[:], in_=stg[p][:],
                                           func=AF.Square, bias=0.0, scale=1.0,
                                           accum_out=sumq[:, hc:hc + 1])
                i_sq = sop(f_sq, [dd, dm])
                r_stgS[p].readers += [i_sum, i_sq]
                r_stats[hc].readers += [i_sum, i_sq]

                sdeps = [i_sum, i_sq] + r_stats[hc].write()
                i_m = vop(lambda v, hc=hc: v.tensor_scalar(
                    out=mst[:, hc:hc + 1], in0=sumx[:, hc:hc + 1],
                    scalar1=1.0 / D, scalar2=None, op0=ALU.mult), sdeps)
                i_mm = vop(lambda v, hc=hc: v.tensor_tensor(
                    out=msq[:, hc:hc + 1], in0=mst[:, hc:hc + 1],
                    in1=mst[:, hc:hc + 1], op=ALU.mult), [i_m], strict=True)
                i_var = vop(lambda v, hc=hc: v.scalar_tensor_tensor(
                    out=vst[:, hc:hc + 1], in0=sumq[:, hc:hc + 1],
                    scalar=1.0 / D, in1=msq[:, hc:hc + 1],
                    op0=ALU.mult, op1=ALU.subtract), [i_mm], strict=True)
                i_ln = sop(lambda s_, hc=hc: s_.activation(
                    out=vst[:, hc:hc + 1], in_=vst[:, hc:hc + 1],
                    func=AF.Ln, bias=C_EPS, scale=1.0), [i_var, i_const])
                i_rs = sop(lambda s_, hc=hc: s_.activation(
                    out=rst[:, hc:hc + 1], in_=vst[:, hc:hc + 1],
                    func=AF.Exp, bias=0.0, scale=C_NHALF),
                    [i_ln], strict=True)
                r_stats[hc].writers = [i_m, i_mm, i_var, i_ln, i_rs]

                pt = hc % 2
                i_ap = vop(lambda v, p=p, pt=pt, hc=hc:
                           v.tensor_scalar(out=xn_tok[pt][:], in0=stg[p][:],
                                           scalar1=mst[:, hc:hc + 1],
                                           scalar2=rst[:, hc:hc + 1],
                                           op0=ALU.subtract, op1=ALU.mult),
                           [i_rs, dd, dm] + r_xn_tok[pt].write(), strict=True)
                r_xn_tok[pt].writers = [i_ap]
                r_stgS[p].readers.append(i_ap)
                r_stats[hc].readers.append(i_ap)
                if v0 > r0:
                    i_ap = vop(lambda v, pt=pt, n=v0 - r0:
                               v.memset(xn_tok[pt][0:n, :], 0.0), [i_ap])
                    r_xn_tok[pt].writers.append(i_ap)
                if v1 < r1:
                    i_ap = vop(lambda v, pt=pt, n0=v1 - r0, n1=r1 - r0:
                               v.memset(xn_tok[pt][n0:n1, :], 0.0), [i_ap])
                    r_xn_tok[pt].writers.append(i_ap)
                a_state[hc] = i_ap

            def th_a_tr(hc):
                p = hc % 2
                i_ap = a_state[hc]
                dw = bank[6].write()
                lastT = None
                for dch in range(8):
                    lastT = top(lambda tt, p=p, dch=dch:
                                tt.transpose(
                                    out=pxnt[:, 128 * dch:128 * (dch + 1)],
                                    in_=xn_tok[p][:, 128 * dch:128 * (dch + 1)],
                                    identity=idb[:]),
                                ([i_ap, d_idb] + dw) if dch == 0 else [])
                bank[6].writers = [lastT]
                r_xn_tok[p].readers.append(lastT)

                ncol = 128 if hc < HCH - 1 else 64

                def f_ev(engs, hc=hc, ncol=ncol, xnf=xnf):
                    srcv = pxnt[:].rearrange("p (c z) -> p c z", c=8)[:, :, 0:ncol]
                    dstv = xnf[:].rearrange("p (c n) -> p c n", c=8)[
                        :, :, 128 * hc:128 * hc + ncol]
                    return engs.tensor_copy(out=dstv, in_=srcv)
                i_ev = vop(f_ev, [lastT] + r_xn_fm[hc].write())
                r_xn_fm[hc].writers = [i_ev]
                bank[6].readers.append(i_ev)

            a_state = {}
            for hc in range(HCH):
                thunks.append(lambda hc=hc: th_a(hc))
            for hc in range(HCH):
                thunks.append(lambda hc=hc: th_a_tr(hc))

            def th_a2(cs):
                for c in cs:
                    p = (HCH + c) % 4
                    deps = r_stgS[p].write()
                    dd = dma_in(stg[p][:],
                                x_d[a + 128 * c:a + 128 * (c + 1), :], deps,
                                sem=SEMS_IN[p])
                    r_stgS[p].writers = [dd]
                    i_c = sop(lambda engs, p=p, c=c, xbf=xbf:
                              engs.activation(out=xbf[:, D * c:D * (c + 1)],
                                              in_=stg[p][:], func=AF.Copy,
                                              bias=0.0, scale=1.0),
                              [dd] + r_xbf[c].write())
                    r_xbf[c].writers = [i_c]
                    r_stgS[p].readers.append(i_c)
            for cc_ in range(CH):
                thunks.append(lambda cc_=cc_: th_a2([cc_]))

            def th_vproj(vc):
                xn3v = xnf[:].rearrange("p (c n) -> p c n", c=8)
                xn_deps = list(r_xn_fm[vc].writers)
                war_v = r_v[pg][vc].write()
                nrow = 128 if vc < HCH - 1 else 64
                dw = bank[6].write() + bank[7].write()
                lastT = None
                for j in range(8):
                    for h2 in range(2):
                        view = pyf[0:nrow, 512 * h2:512 * (h2 + 1)]
                        lastT = top(
                            lambda tt, vc=vc, j=j, view=view, nrow=nrow,
                            xn3v=xn3v, h2=h2:
                            tt.matmul(
                                out=view,
                                lhsT=xn3v[:, j, 128 * vc:128 * vc + nrow],
                                rhs=wsb["v"][:, D * j + 512 * h2:
                                             D * j + 512 * (h2 + 1)],
                                start=(j == 0), stop=False),
                            (dw + xn_deps + ev_w["v"])
                            if (j == 0 and h2 == 0) else [])
                for h2 in range(2):
                    view = pyf[0:nrow, 512 * h2:512 * (h2 + 1)]
                    lastT = top(lambda tt, h2=h2, view=view, nrow=nrow:
                                tt.matmul(out=view, lhsT=ones_bf[:, 0:nrow],
                                          rhs=bv_bf[:, 512 * h2:512 * (h2 + 1)],
                                          start=False, stop=True),
                                [i_ones, i_r1])
                bank[6].writers = [lastT]
                bank[7].writers = [lastT]
                r_xn_fm[vc].readers.append(lastT)

                e0 = sop(lambda engs, vc=vc, nrow=nrow, vtk=vtk:
                         engs.activation(
                             out=vtk[0:nrow, D * vc:D * vc + 512],
                             in_=pyf[0:nrow, 0:512], func=AF.Copy,
                             bias=0.0, scale=1.0), [lastT] + war_v)
                e1 = sop(lambda engs, vc=vc, nrow=nrow, vtk=vtk:
                         engs.activation(
                             out=vtk[0:nrow, D * vc + 512:D * (vc + 1)],
                             in_=pyf[0:nrow, 512:1024], func=AF.Copy,
                             bias=0.0, scale=1.0), [lastT])
                bank[6].readers.append(e0)
                bank[7].readers.append(e1)
                r_v[pg][vc].writers = [e0, e1]

            for vc in range(HCH):
                thunks.append(lambda vc=vc: th_vproj(vc))

            def th_qk(dst64, wname, col0, ncols, bias_s, scale, res, irng):
                xn3v = xnf[:].rearrange("p (c n) -> p c n", c=8)
                xn_deps = [w_ for rr in r_xn_fm for w_ in rr.writers]
                war = res.write()
                stripes = []
                s0 = 0
                while s0 < ncols:
                    sl = min(512, ncols - s0)
                    stripes.append((s0, sl))
                    s0 += sl
                for i in irng:
                    qkt = qktmp[i % 2]
                    wart = r_qktmp[i % 2].write()
                    evs_i = []
                    dw = bank[6].write() + (bank[7].write()
                                            if len(stripes) > 1 else [])
                    lastT = None
                    for j in range(8):
                        for si, (s0, sl) in enumerate(stripes):
                            view = pyf[:, 512 * si:512 * si + sl]
                            lastT = top(
                                lambda tt, i=i, j=j, s0=s0, sl=sl, view=view,
                                wname=wname, col0=col0, xn3v=xn3v:
                                tt.matmul(
                                    out=view,
                                    lhsT=wsb[wname][:, D * j + 128 * i:
                                                    D * j + 128 * (i + 1)],
                                    rhs=xn3v[:, j, col0 + s0:col0 + s0 + sl],
                                    start=(j == 0), stop=(j == 7)),
                                (dw + xn_deps + ev_w[wname])
                                if (j == 0 and si == 0) else [])
                    for si, (s0, sl) in enumerate(stripes):
                        view = pyf[:, 512 * si:512 * si + sl]
                        bk_ = 6 + si
                        bank[bk_].writers = [lastT]
                        if i % 2 == 0:
                            ee = sop(lambda engs, s0=s0, sl=sl, view=view,
                                     i=i, bias_s=bias_s, scale=scale, qkt=qkt:
                                     engs.activation(
                                         out=qkt[:, s0:s0 + sl],
                                         in_=view, func=AF.Identity,
                                         bias=bias_s[:, i:i + 1], scale=scale),
                                     [lastT, d_bq, d_bk] + wart)
                        else:
                            ee = vop(lambda v, s0=s0, sl=sl, view=view,
                                     i=i, bias_s=bias_s, scale=scale, qkt=qkt:
                                     v.tensor_scalar(
                                         out=qkt[:, s0:s0 + sl], in0=view,
                                         scalar1=scale,
                                         scalar2=bias_s[:, i:i + 1],
                                         op0=ALU.mult, op1=ALU.add),
                                     [lastT, d_bq, d_bk, i_const] + wart,
                                     strict=True)
                        bank[bk_].readers.append(ee)
                        evs_i.append(ee)
                    qsem = "QK" if i % 2 == 0 else "QK2"
                    d_even = dma_s(dst64[0:64, ncols * (2 * i):
                                         ncols * (2 * i + 1)],
                                   qkt[0:64, 0:ncols],
                                   evs_i + war, sem=qsem)
                    d_odd = dma_s(dst64[0:64, ncols * (2 * i + 1):
                                        ncols * (2 * i + 2)],
                                  qkt[64:128, 0:ncols],
                                  evs_i, sem=qsem)
                    r_qktmp[i % 2].writers = evs_i
                    r_qktmp[i % 2].readers = [d_even, d_odd]
                    res.writers += [d_even, d_odd]
                for rr in r_xn_fm:
                    rr.readers.append(lastT)

            thunks.append(lambda: th_qk(q64, "q", HB, TILE, bq_s, C_QS,
                                        r_q, (0, 1, 2, 3)))
            thunks.append(lambda: th_qk(k64, "k", 0, HTOK, bk_s, 1.0,
                                        r_k, (0, 1, 2, 3)))
            thunks.append(lambda: th_qk(q64, "q", HB, TILE, bq_s, C_QS,
                                        r_q, (4, 5, 6, 7)))
            thunks.append(lambda: th_qk(k64, "k", 0, HTOK, bk_s, 1.0,
                                        r_k, (4, 5, 6, 7)))
            return thunks

        def emit_qk(c, w, pg):
            bks = BANKW[w]
            pwv = pw[w]
            lastQK = None
            for hh in range(WH):
                h = w * WH + hh
                j0 = WIN * hh
                segs = []
                s0 = j0
                while s0 < j0 + WIN:
                    s1 = min(j0 + WIN, (s0 // 512 + 1) * 512)
                    segs.append((s0, s1))
                    s0 = s1
                for (s0, s1) in segs:
                    bno = bks[s0 // 512]
                    deps = bank[bno].write() + \
                        (r_q.read() + r_k.read() if lastQK is None else [])
                    lastQK = top(
                        lambda tt, h=h, s0=s0, s1=s1, c=c, pwv=pwv, j0=j0:
                        tt.matmul(
                            out=pwv[:, s0:s1],
                            lhsT=q64[0:64, TILE * h + 128 * c:
                                     TILE * h + 128 * (c + 1)],
                            rhs=k64[0:64,
                                    HTOK * h + 128 * c + (s0 - j0):
                                    HTOK * h + 128 * c + (s1 - j0)],
                            start=True, stop=True), deps)
                    bank[bno].writers.append(lastQK)
            r_q.readers.append(lastQK)
            r_k.readers.append(lastQK)
            return lastQK

        def emit_sm1(c, w, lastQK, left_edge, right_edge):
            pwv = pw[w]
            bks = BANKW[w]
            i_mx = vop(lambda v, pwv=pwv, w=w:
                       v.reduce_max(out=negm[:, w:w + 1],
                                    in_=pwv[:, 0:1536],
                                    axis=AX.X, negate=True),
                       [lastQK] + r_negm[w].write())
            r_negm[w].writers = [i_mx]
            for b_ in bks:
                bank[b_].readers.append(i_mx)
            return [i_mx]

        def emit_exp(c, w, i_sms):
            pwv = pw[w]
            bks = BANKW[w]
            i_exp = sop(lambda engs, pwv=pwv, w=w:
                        engs.activation(out=attn_t[w][:],
                                        in_=pwv[:, 0:1536],
                                        func=AF.Exp,
                                        bias=negm[:, w:w + 1],
                                        scale=1.0),
                        i_sms + r_attn[w].write())
            r_attn[w].writers = [i_exp]
            r_negm[w].readers.append(i_exp)
            for b_ in bks:
                bank[b_].readers.append(i_exp)
            return i_exp

        def emit_av(c, w, i_exp, pg, left_edge, right_edge):
            pwv = pw[w]
            bks = BANKW[w]
            vtk = v_tok[pg]
            mv = 1 if (left_edge and c == 0) else \
                (2 if (right_edge and c == CH - 1) else 0)
            i_msk = vop(lambda v, w=w, mv=mv:
                        v.tensor_tensor(
                            out=attn_t[w][:].rearrange(
                                "p (h j) -> p h j", h=WH),
                            in0=attn_t[w][:].rearrange(
                                "p (h j) -> p h j", h=WH),
                            in1=masks[:, WIN * mv:WIN * (mv + 1)]
                            .unsqueeze(1).broadcast_to([128, WH, WIN]),
                            op=ALU.mult), [i_exp, d_mask])
            r_attn[w].writers.append(i_msk)
            i_zs = vop(lambda v, w=w:
                       v.reduce_sum(out=zsum[:, 8 * w:8 * w + 8],
                                    in_=attn_t[w][:].rearrange(
                                        "p (h j) -> p h j", h=WH),
                                    axis=AX.X), [i_msk], strict=True)
            i_eps = vop(lambda v, w=w:
                        v.tensor_scalar(out=zsum[:, 8 * w:8 * w + 8],
                                        in0=zsum[:, 8 * w:8 * w + 8],
                                        scalar1=EPS_Z, scalar2=None,
                                        op0=ALU.add), [i_zs], strict=True)
            i_rc = vop(lambda v, w=w:
                       v.reciprocal(out=rcp[:, 8 * w:8 * w + 8],
                                    in_=zsum[:, 8 * w:8 * w + 8]),
                       [i_eps], strict=True)
            i_nrm = []
            for hh in range(WH):
                i_nrm.append(
                    vop(lambda v, hh=hh, w=w:
                        v.tensor_scalar(
                            out=attn_t[w][:, WIN * hh:WIN * (hh + 1)],
                            in0=attn_t[w][:, WIN * hh:WIN * (hh + 1)],
                            scalar1=rcp[:, 8 * w + hh:8 * w + hh + 1],
                            scalar2=None, op0=ALU.mult),
                        [i_rc], strict=True))
            r_attn[w].writers += i_nrm

            t1v = pwv[:, 0:512].bitcast(BF16)
            t2v = pwv[:, 512:1024].bitcast(BF16)
            dwT = bank[bks[0]].write() + bank[bks[1]].write()
            lastTT = None
            for hh in range(WH):
                lastTT = top(
                    lambda tt, hh=hh, t1v=t1v, w=w:
                    tt.transpose(out=t1v[:, 128 * hh:128 * (hh + 1)],
                                 in_=attn_t[w][:, WIN * hh:WIN * hh + 128],
                                 identity=idb[:]),
                    (i_nrm + dwT) if hh == 0 else [])
                lastTT = top(
                    lambda tt, hh=hh, t2v=t2v, w=w:
                    tt.transpose(out=t2v[0:64, 128 * hh:128 * (hh + 1)],
                                 in_=attn_t[w][:, WIN * hh + 128:
                                                 WIN * hh + 192],
                                 identity=idb[:]), [])
            bank[bks[0]].writers = [lastTT]
            bank[bks[1]].writers = [lastTT]
            r_attn[w].readers.append(lastTT)

            i_e1 = vop(lambda v, w=w, t1v=t1v:
                       v.tensor_copy(out=at1[w][:], in_=t1v[:, 0:1024]),
                       [lastTT] + r_at[w].write())
            i_e2 = sop(lambda engs, w=w, t2v=t2v:
                       engs.activation(out=at2[w][:],
                                       in_=t2v[0:64, 0:1024],
                                       func=AF.Copy, bias=0.0,
                                       scale=1.0),
                       [lastTT])
            r_at[w].writers = [i_e1, i_e2]
            bank[bks[0]].readers.append(i_e1)
            bank[bks[1]].readers.append(i_e2)

            cview = pwv[:, 1024:1536]
            dwC = bank[bks[2]].write()
            lastAV = None
            for hh in range(WH):
                h = w * WH + hh
                deps = []
                if hh == 0:
                    deps = dwC + [i_e1, i_e2] + r_v[pg][c].read() \
                        + r_v[pg][c + 1].read()
                lastAV = top(
                    lambda tt, hh=hh, h=h, c=c, w=w, cview=cview, vtk=vtk:
                    tt.matmul(
                        out=cview[64 * (hh % 2):64 * (hh % 2) + 64,
                                  128 * (hh // 2):128 * (hh // 2) + 128],
                        lhsT=vtk[:, D * c + 64 * h:D * c + 64 * (h + 1)],
                        rhs=at1[w][:, 128 * hh:128 * (hh + 1)],
                        start=True, stop=False), deps)
                lastAV = top(
                    lambda tt, hh=hh, h=h, c=c, w=w, cview=cview, vtk=vtk:
                    tt.matmul(
                        out=cview[64 * (hh % 2):64 * (hh % 2) + 64,
                                  128 * (hh // 2):128 * (hh // 2) + 128],
                        lhsT=vtk[0:64, D * (c + 1) + 64 * h:
                                 D * (c + 1) + 64 * (h + 1)],
                        rhs=at2[w][:, 128 * hh:128 * (hh + 1)],
                        start=False, stop=True), [])
            bank[bks[2]].writers = [lastAV]
            r_at[w].readers.append(lastAV)
            r_v[pg][c].readers.append(lastAV)
            r_v[pg][c + 1].readers.append(lastAV)

            i_ce = sop(lambda engs, w=w, cview=cview, c=c:
                       engs.activation(out=ctx[:, 512 * w:512 * (w + 1)],
                                       in_=cview, func=AF.Copy,
                                       bias=0.0, scale=1.0),
                       [lastAV] + (r_ctx.write() if w == 0 else []))
            r_ctx.writers.append(i_ce)
            bank[bks[2]].readers.append(i_ce)
            return i_ce

        def ln_post(src_view, src_dep, bank_ids, dst_bf, r_dst, col, r_st):
            i_cp = sop(lambda engs: engs.activation(
                out=dst_bf[:], in_=src_view, func=AF.Copy, bias=0.0,
                scale=1.0, accum_out=sumx[:, col:col + 1]),
                [src_dep] + r_dst.write() + r_st.read())
            r_dst.writers = [i_cp]
            i_sq2 = sop(lambda engs: engs.activation(
                out=scrap[:], in_=dst_bf[:], func=AF.Square, bias=0.0,
                scale=1.0, accum_out=sumq[:, col:col + 1]),
                [i_cp], strict=True)
            r_dst.readers.append(i_sq2)
            for b_ in bank_ids:
                bank[b_].readers.append(i_cp)
            sdep = [i_cp, i_sq2] + r_st.write()
            i_m2 = vop(lambda v: v.tensor_scalar(
                out=mst[:, col:col + 1], in0=sumx[:, col:col + 1],
                scalar1=1.0 / D, scalar2=None, op0=ALU.mult), sdep)
            i_mm2 = vop(lambda v: v.tensor_tensor(
                out=msq[:, col:col + 1], in0=mst[:, col:col + 1],
                in1=mst[:, col:col + 1], op=ALU.mult), [i_m2], strict=True)
            i_v2 = vop(lambda v: v.scalar_tensor_tensor(
                out=vst[:, col:col + 1], in0=sumq[:, col:col + 1],
                scalar=1.0 / D, in1=msq[:, col:col + 1],
                op0=ALU.mult, op1=ALU.subtract), [i_mm2], strict=True)
            i_l2 = sop(lambda engs: engs.activation(
                out=vst[:, col:col + 1], in_=vst[:, col:col + 1],
                func=AF.Ln, bias=C_EPS, scale=1.0), [i_v2])
            i_r2x = sop(lambda engs: engs.activation(
                out=rst[:, col:col + 1], in_=vst[:, col:col + 1],
                func=AF.Exp, bias=0.0, scale=C_NHALF),
                [i_l2], strict=True)
            r_st.writers = [i_m2, i_mm2, i_v2, i_l2, i_r2x]
            return i_cp, i_r2x

        def emit_d1(t, c, ctx_done, a):
            pg = t % 2
            xbf = x_bf
            tb = tbf[c % 2]
            x3f = xn3_f[c % 2]
            r_tb = r_tbf[c % 2]
            r_x3 = r_xn3f[c % 2]
            yv = pyf[:, 0:1024]
            dwY = bank[6].write() + bank[7].write()
            lastWO = None
            for pch in range(8):
                for h2 in range(2):
                    deps = (dwY + ctx_done + ev_w["o"]) \
                        if (pch == 0 and h2 == 0) else []
                    lastWO = top(
                        lambda tt, pch=pch, h2=h2, yv=yv, c=c:
                        tt.matmul(
                            out=yv[:, 512 * h2:512 * (h2 + 1)],
                            lhsT=ctx[:, 128 * pch:128 * (pch + 1)],
                            rhs=wsb["o"][:, D * pch + 512 * h2:
                                         D * pch + 512 * (h2 + 1)],
                            start=(pch == 0), stop=False), deps)
            for h2 in range(2):
                lastWO = top(lambda tt, h2=h2, yv=yv:
                             tt.matmul(out=yv[:, 512 * h2:512 * (h2 + 1)],
                                       lhsT=ones_bf[:],
                                       rhs=bo_bf[:, 512 * h2:
                                                 512 * (h2 + 1)],
                                       start=False, stop=True),
                             [i_r2] if (t == 0 and c == 0) else [])
            bank[6].writers = [lastWO]
            bank[7].writers = [lastWO]
            r_ctx.readers.append(lastWO)

            i_ycp, i_yrs = ln_post(yv, lastWO, [6, 7], tb, r_tb, 10, r_st2)
            i_t1 = vop(lambda v, tb=tb: v.tensor_scalar(
                out=tb[:], in0=tb[:], scalar1=mst[:, 10:11],
                scalar2=rst[:, 10:11], op0=ALU.subtract, op1=ALU.mult),
                [i_yrs, i_ycp], strict=True)
            r_tb.writers.append(i_t1)
            r_st2.readers.append(i_t1)

            i_x2 = vop(lambda v, c=c, xbf=xbf, tb=tb: v.tensor_tensor(
                out=xbf[:, D * c:D * (c + 1)],
                in0=xbf[:, D * c:D * (c + 1)], in1=tb[:], op=ALU.add),
                [i_t1] + r_xbf[c].write())
            r_xbf[c].writers = [i_x2]
            r_tb.readers.append(i_x2)

            i_s3 = vop(lambda v, c=c, xbf=xbf: v.reduce_sum(
                out=sumx[:, 11:12], in_=xbf[:, D * c:D * (c + 1)],
                axis=AX.X), [i_x2] + r_st3.read())
            i_q3 = sop(lambda engs, c=c, xbf=xbf: engs.activation(
                out=scrap[:], in_=xbf[:, D * c:D * (c + 1)],
                func=AF.Square, bias=0.0, scale=1.0,
                accum_out=sumq[:, 11:12]), [i_x2])
            r_xbf[c].readers += [i_s3, i_q3]
            sdep = [i_s3, i_q3] + r_st3.write()
            i_m3 = vop(lambda v: v.tensor_scalar(
                out=mst[:, 11:12], in0=sumx[:, 11:12], scalar1=1.0 / D,
                scalar2=None, op0=ALU.mult), sdep)
            i_mm3 = vop(lambda v: v.tensor_tensor(
                out=msq[:, 11:12], in0=mst[:, 11:12], in1=mst[:, 11:12],
                op=ALU.mult), [i_m3], strict=True)
            i_v3 = vop(lambda v: v.scalar_tensor_tensor(
                out=vst[:, 11:12], in0=sumq[:, 11:12], scalar=1.0 / D,
                in1=msq[:, 11:12], op0=ALU.mult, op1=ALU.subtract),
                [i_mm3], strict=True)
            i_l3 = sop(lambda engs: engs.activation(
                out=vst[:, 11:12], in_=vst[:, 11:12], func=AF.Ln,
                bias=C_EPS, scale=1.0), [i_v3])
            i_r3x = sop(lambda engs: engs.activation(
                out=rst[:, 11:12], in_=vst[:, 11:12], func=AF.Exp,
                bias=0.0, scale=C_NHALF), [i_l3], strict=True)
            r_st3.writers = [i_m3, i_mm3, i_v3, i_l3, i_r3x]

            i_a3 = vop(lambda v, c=c, xbf=xbf: v.tensor_scalar(
                out=xn3_t[:], in0=xbf[:, D * c:D * (c + 1)],
                scalar1=mst[:, 11:12], scalar2=rst[:, 11:12],
                op0=ALU.subtract, op1=ALU.mult),
                [i_r3x] + r_xn3t.write(), strict=True)
            r_xn3t.writers = [i_a3]
            r_xbf[c].readers.append(i_a3)
            r_st3.readers.append(i_a3)

            return i_a3

        def emit_d1b(t, c, i_a3):
            x3f = xn3_f[c % 2]
            r_x3 = r_xn3f[c % 2]
            dw6 = bank[6].write()
            lastT3 = None
            for dch in range(8):
                lastT3 = top(lambda tt, dch=dch:
                             tt.transpose(
                                 out=pxnt[:, 128 * dch:128 * (dch + 1)],
                                 in_=xn3_t[:, 128 * dch:128 * (dch + 1)],
                                 identity=idb[:]),
                             ([i_a3] + dw6) if dch == 0 else [])
            bank[6].writers = [lastT3]
            r_xn3t.readers.append(lastT3)
            i_e3 = vop(lambda v, x3f=x3f: v.tensor_copy(
                out=x3f[:].rearrange("p (c n) -> p c n", c=8),
                in_=pxnt[:].rearrange("p (c z) -> p c z", c=8)),
                [lastT3] + r_x3.write())
            r_x3.writers = [i_e3]
            bank[6].readers.append(i_e3)
            return i_e3

        def emit_d2(t, c, a):
            pg = t % 2
            xbf = x_bf
            tb = tbf[c % 2]
            x3f = xn3_f[c % 2]
            r_tb = r_tbf[c % 2]
            r_x3 = r_xn3f[c % 2]
            i_e3 = r_x3.writers[0] if r_x3.writers else None
            fv = pyf[:, 0:1024]
            dwF = bank[6].write() + bank[7].write()
            lastF = None
            for pch in range(8):
                for h2 in range(2):
                    deps = (dwF + ([i_e3] if i_e3 is not None else []) +
                            (ev_m if (t == 0 and c == 0) else [])) \
                        if (pch == 0 and h2 == 0) else []
                    lastF = top(
                        lambda tt, pch=pch, h2=h2, fv=fv, x3f=x3f:
                        tt.matmul(
                            out=fv[:, 512 * h2:512 * (h2 + 1)],
                            lhsT=x3f[:, 128 * pch:128 * (pch + 1)],
                            rhs=m_sb[:, D * pch + 512 * h2:
                                     D * pch + 512 * (h2 + 1)],
                            start=(pch == 0), stop=False), deps)
            for h2 in range(2):
                lastF = top(lambda tt, h2=h2, fv=fv:
                            tt.matmul(out=fv[:, 512 * h2:512 * (h2 + 1)],
                                      lhsT=ones_bf[:],
                                      rhs=bf_bf[:, 512 * h2:
                                                512 * (h2 + 1)],
                                      start=False, stop=True),
                            [i_r3] if (t == 0 and c == 0) else [])
            bank[6].writers = [lastF]
            bank[7].writers = [lastF]
            r_x3.readers.append(lastF)

            i_fcp, i_frs = ln_post(fv, lastF, [6, 7], tb, r_tb, 12, r_st4)
            i_t2 = vop(lambda v, tb=tb: v.tensor_scalar(
                out=tb[:], in0=tb[:], scalar1=mst[:, 12:13],
                scalar2=rst[:, 12:13], op0=ALU.subtract, op1=ALU.mult),
                [i_frs, i_fcp], strict=True)
            r_tb.writers.append(i_t2)
            r_st4.readers.append(i_t2)

            i_of = vop(lambda v, c=c, xbf=xbf, tb=tb: v.tensor_tensor(
                out=outsb[:], in0=xbf[:, D * c:D * (c + 1)], in1=tb[:],
                op=ALU.add), [i_t2] + r_out.write())
            r_out.writers = [i_of]
            r_tb.readers.append(i_of)
            r_xbf[c].readers.append(i_of)

            dd = dma_g(y_d[a + 128 * c:a + 128 * (c + 1), :], outsb[:],
                       [i_of])
            r_out.readers.append(dd)
            return lastF

        SCHED = {0: [0, 1], 1: [5, 6, 2, 3, 14, 15],
                 2: [7, 8, 4, 16, 17, 10],
                 3: [9, 19, 20, 21, 22, 18, 11]}
        p0 = make_preps(0)
        for ix in (0, 1, 5, 2, 6, 3, 7, 4, 8, 9, 10, 11, 12, 13,
                   14, 15, 16, 17, 18, 19, 20, 21, 22):
            p0[ix]()
        pend1 = None
        pend2 = None
        carry = None
        pre = None
        for t in range(NTILE):
            a = t * TILE
            seq_lo = (a // S) * S
            left_edge = (a == seq_lo)
            right_edge = (a + TILE == seq_lo + S)
            nxt = make_preps(t + 1) if t + 1 < NTILE else None
            for c in range(CH):
                qkA = emit_qk(c, 0, t % 2)
                qkB = emit_qk(c, 1, t % 2)
                smA = emit_sm1(c, 0, qkA, left_edge, right_edge)
                smB = emit_sm1(c, 1, qkB, left_edge, right_edge)
                expA = emit_exp(c, 0, smA)
                expB = emit_exp(c, 1, smB)
                if pend2 is not None:
                    emit_d2(pend2[0], pend2[1], pend2[3])
                    pend2 = None
                pend_b = None
                if pend1 is not None:
                    i_a3_ = emit_d1(*pend1)
                    pend_b = (pend1[0], pend1[1], i_a3_)
                    pend2 = pend1
                    pend1 = None
                ceA = emit_av(c, 0, expA, t % 2, left_edge, right_edge)
                ceB = emit_av(c, 1, expB, t % 2, left_edge, right_edge)
                pend1 = (t, c, [ceA, ceB], a)
                if c == 0 and carry is not None:
                    carry[0]()
                if c == 1 and carry is not None:
                    carry[1]()
                    carry = None
                if nxt is not None:
                    for ix in SCHED[c]:
                        nxt[ix]()
                if pend_b is not None:
                    emit_d1b(*pend_b)
            if nxt is not None:
                carry = (nxt[12], nxt[13])
            MARKS["t%d_end" % t] = len(tr.ops)
        emit_d2(pend2[0], pend2[1], pend2[3])
        i_a3_ = emit_d1(*pend1)
        emit_d1b(pend1[0], pend1[1], i_a3_)
        emit_d2(pend1[0], pend1[1], pend1[3])

        kstop = os.environ.get("KSTOP")
        if kstop:
            nmax = MARKS[kstop] if kstop in MARKS else int(kstop)
            tr.ops = tr.ops[:nmax]
            print("KSTOP at", nmax, "ops")

        with (nc.semaphore("semT") as semT, nc.semaphore("semV") as semV,
              nc.semaphore("semS") as semS, nc.semaphore("semG") as semG,
              nc.semaphore("semIN") as semIN, nc.semaphore("semOUT") as semOUT,
              nc.semaphore("semIN2") as semIN2,
              nc.semaphore("semIN3") as semIN3,
              nc.semaphore("semIN4") as semIN4,
              nc.semaphore("semQK") as semQK,
              nc.semaphore("semQK2") as semQK2,
              nc.Block() as block):
            sems = dict(T=semT, V=semV, S=semS, G=semG, IN=semIN, OUT=semOUT,
                        IN2=semIN2, IN3=semIN3, IN4=semIN4, QK=semQK,
                        QK2=semQK2)
            inc_no = {}
            cnt = {k: 0 for k in
                   ("T", "V", "S", "G", "IN", "IN2", "IN3", "IN4",
                    "OUT", "QK", "QK2")}
            for i, o in enumerate(tr.ops):
                if o["dma"] is not None:
                    cnt[o["dma"]] += 16
                    inc_no[i] = (o["dma"], cnt[o["dma"]])
                elif i in tr.need_inc:
                    cnt[o["eng"]] += 1
                    inc_no[i] = (o["eng"], cnt[o["eng"]])
            total_out = cnt["OUT"]

            def emit(eng_name, engine):
                waited = {}
                for i, o in enumerate(tr.ops):
                    if o["eng"] != eng_name:
                        continue
                    for d in o["deps"]:
                        tok = inc_no.get(d)
                        if tok is None:
                            continue
                        sname, val = tok
                        if (sname == eng_name and tr.ops[d]["dma"] is None
                                and not o["strict"]
                                and not os.environ.get("KSTRICT")
                                and eng_name == "T"):
                            continue
                        if waited.get(sname, 0) >= val:
                            continue
                        engine.wait_ge(sems[sname], val)
                        waited[sname] = val
                    ins = o["fn"](engine)
                    tok = inc_no.get(i)
                    if tok is not None:
                        ins.then_inc(sems[tok[0]], 16 if o["dma"] else 1)
                if eng_name == "Y" and total_out:
                    engine.wait_ge(sems["OUT"], total_out)

            @block.tensor
            def _(engine):
                emit("T", engine)

            @block.vector
            def _(engine):
                emit("V", engine)

            @block.scalar
            def _(engine):
                emit("S", engine)

            @block.gpsimd
            def _(engine):
                emit("G", engine)

            @block.sync
            def _(engine):
                emit("Y", engine)

    return nc


def host_inputs(inp):
    x = np.ascontiguousarray(inp["L0_attn_x"], np.float32).reshape(B * S, D)
    pq_ = np.arange(D, dtype=np.int64)
    mm = (pq_[:, None] * pq_[None, :]) % D
    ang = (2.0 * np.pi / D) * mm
    Cp = (np.cos(ang) / 32.0).astype(np.float32).astype(ml_dtypes.bfloat16)
    Sn = (-np.sin(ang) / 32.0).astype(np.float32).astype(ml_dtypes.bfloat16)

    j = np.arange(WIN)[None, :]
    q = np.arange(128)[:, None]
    band = (j - 32 - q >= -32) & (j - 32 - q <= 32)
    mid = np.where(band, 1.0, 0.0).astype(np.float32)
    left = np.where(band & (j >= 32), 1.0, 0.0).astype(np.float32)
    right = np.where(band & (j < 160), 1.0, 0.0).astype(np.float32)
    masks = np.concatenate([mid, left, right],
                           axis=1).astype(ml_dtypes.bfloat16)

    rows = np.zeros((65, D), np.float32)
    rows[0] = inp["L0_attn_bv"]
    rows[32] = inp["L0_attn_bo"]
    rows[64] = inp["L0_ffn_bias"]

    base = {
        "wq": np.ascontiguousarray(inp["L0_attn_Wq"].T, np.float32),
        "wk": np.ascontiguousarray(inp["L0_attn_Wk"].T, np.float32),
        "wv": np.ascontiguousarray(inp["L0_attn_Wv"].T, np.float32),
        "wo": np.ascontiguousarray(inp["L0_attn_Wo"].T, np.float32),
        "bq": np.ascontiguousarray(
            (inp["L0_attn_bq"].astype(np.float32) * 0.125).reshape(8, 128).T),
        "bk": np.ascontiguousarray(
            inp["L0_attn_bk"].astype(np.float32).reshape(8, 128).T),
        "rows": rows,
        "Cp": Cp,
        "Sn": Sn,
        "wrT": np.ascontiguousarray(inp["L0_ffn_w_freq_real"].T, np.float32),
        "wiT": np.ascontiguousarray(inp["L0_ffn_w_freq_imag"].T, np.float32),
        "masks": masks,
        "idb": np.eye(128).astype(ml_dtypes.bfloat16),
    }
    maps = []
    for c in range(NCORES):
        m = dict(base)
        m["x"] = np.ascontiguousarray(x[c * TOK:(c + 1) * TOK])
        maps.append(m)
    return maps


def kernel(**inputs):
    maps = host_inputs(inputs)
    nc = build_program()
    res = run_bass_kernel_spmd(nc, maps, core_ids=list(range(NCORES)))
    out = np.concatenate([np.asarray(r["y"], np.float32)
                          for r in res.results], axis=0)
    return out.reshape(B, S, D)
